# version 09
# baseline (speedup 1.0000x reference)
import numpy as np
import ml_dtypes
from contextlib import ExitStack

import concourse.bass as bass
import concourse.mybir as mybir
from concourse.bass_utils import run_bass_kernel_spmd

F32 = mybir.dt.float32
BF16 = mybir.dt.bfloat16
I32 = mybir.dt.int32
AF = mybir.ActivationFunctionType
ALU = mybir.AluOpType

D = 2048
L = 2048
NB = 2
T = NB * L
HID = 5632
DI = 1536
H = 24
S = 64
DH = 64
P3R = 3200
KD = D // 128
KH = HID // 128
KI = DI // 128
KP = P3R // 128
CHK = 128
NCH = T // CHK
PADL = 3
XPW = PADL + L
LN_EPS = 1e-5
RMS_EPS = float(np.finfo(np.float32).eps)
LN2 = float(np.log(2.0))

CC_M128, CC_ONE, CC_LNEPS, CC_RMSB = 0, 1, 2, 3
NCC = 4

PE_A, PE_H = 9, 5
ACT_A = 22
DVE_A, DVE_H = 15, 7
PE_PC = PE_A + 24 * PE_H
ACT_PC = 118
DVE_PC = DVE_A + 24 * DVE_H
PO_PC = 51
LD_PC = 144


def ACT_rtc(h):
    return 23 if h == 0 else 20 + 4 * h


def ACT_dexp(h):
    return 116 if h == 23 else 25 + 4 * h


def ACT_w2e(h):
    return 117 if h == 23 else 26 + 4 * h


def ACT_ece(h):
    return 118 if h == 23 else 27 + 4 * h


def PO_ctl(h):
    return 4 if h == 0 else 2 * h + 3


def PO_yw(h):
    return 51 if h == 23 else 2 * h + 6


class Sem:
    def __init__(self, ctx, nc, name):
        self.h = ctx.enter_context(nc.semaphore(name))
        self.v = 0

    def inc(self, instr, n=1):
        instr.then_inc(self.h, n)
        self.v += n


def build_program(phases_upto=99, dev=False, zset=(), scan_np=None, scan_jc=None):
    nc = bass.Bass("TRN2", target_bir_lowering=False)

    def din(name, shape, dt):
        return nc.dram_tensor(name, shape, dt, kind="ExternalInput").ap()

    hsT = din("hsT", [D, T], F32)
    wgT = din("wgT", [D, HID], I32)
    wuT = din("wuT", [D, HID], I32)
    wdT = din("wdT", [HID, D], I32)
    sgate = din("sgate", [128, KH], F32)
    sup = din("sup", [128, KH], F32)
    sdown = din("sdown", [128, KD], F32)
    inT = din("inT", [D, 2 * DI], F32)
    inb = din("inb", [128, 24], F32)
    xpTw = din("xpTw", [DI, P3R], F32)
    xpb = din("xpb", [128, KP], F32)
    opT = din("opT", [DI, D], F32)
    opb = din("opb", [128, KD], F32)
    convw = din("convw", [128, KI * 4], F32)
    convb = din("convb", [128, KI], F32)
    normw = din("normw", [128, KD], F32)
    indic_h = din("indic_h", [24, DI], F32)
    headv = din("headv", [24, 4], F32)
    resg = din("resg", [1, 1], F32)
    ones_h = din("ones_h", [128, 128], F32)
    indicT_h = din("indicT_h", [128, KI * 24], F32)
    indicT64_h = din("indicT64_h", [128, KI * 24], F32)
    maskpos_h = din("maskpos_h", [128, 128], F32)
    ident_h = din("ident_h", [128, 128], F32)
    ccol_h = din("ccol_h", [128, NCC], F32)

    out = nc.dram_tensor("out", [D, T], F32, kind="ExternalOutput").ap()
    def dk(name):
        return "ExternalOutput" if (dev or name in zset) else "Internal"
    resA = nc.dram_tensor("resA", [D, T], F32, kind=dk("resA")).ap()
    resB = nc.dram_tensor("resB", [D, T], F32, kind=dk("resB")).ap()
    g_bf = nc.dram_tensor("g_bf", [HID, T], BF16, kind=dk("g_bf")).ap()
    wd_bf2 = nc.dram_tensor("wd_bf2", [KD, HID, 128], BF16, kind=dk("wd_bf2")).ap()
    xpad = nc.dram_tensor("xpad", [DI, NB * XPW], BF16, kind=dk("xpad")).ap()
    siluzT = nc.dram_tensor("siluzT", [DI, T], BF16, kind=dk("siluzT")).ap()
    xpT_s = nc.dram_tensor("xpT_s", [P3R, T], BF16, kind=dk("xpT_s")).ap()
    qrow_d = nc.dram_tensor("qrow_d", [64, 128], BF16, kind="Internal").ap()
    lnm_d = nc.dram_tensor("lnm_d", [128, T], F32, kind="Internal").ap()
    lnr_d = nc.dram_tensor("lnr_d", [128, T], F32, kind="Internal").ap()
    if dev:
        xln_d = nc.dram_tensor("xln_d", [D, T], BF16, kind="ExternalOutput").ap()
        xcy_d = nc.dram_tensor("xcy_d", [DI, T], BF16, kind="ExternalOutput").ap()

    ctx = ExitStack()
    with ctx:
        sLd = Sem(ctx, nc, "sLd")
        sSc = Sem(ctx, nc, "sSc")
        sWl = Sem(ctx, nc, "sWl")
        sSt = Sem(ctx, nc, "sSt")
        sPE = Sem(ctx, nc, "sPE")
        sACT = Sem(ctx, nc, "sACT")
        sDVE = Sem(ctx, nc, "sDVE")
        sPool = Sem(ctx, nc, "sPool")
        sQ = [Sem(ctx, nc, "sQ0"), Sem(ctx, nc, "sQ1")]
        sR = [Sem(ctx, nc, "sR0"), Sem(ctx, nc, "sR1")]
        ALLS = [sLd, sSc, sWl, sSt, sPE, sACT, sDVE, sPool] + sQ + sR

        def snap():
            return [(s, s.v) for s in ALLS]

        def bwait(eng, sn):
            for s, v in sn:
                if v > 0:
                    eng.wait_ge(s.h, v)

        sb = lambda name, shape, dt=F32: ctx.enter_context(nc.sbuf_tensor(name, shape, dt))
        xln = sb("xln", [128, KD * 4096], BF16)
        ones_b = sb("ones_b", [128, 128], BF16)
        indic_b = sb("indic_b", [24, DI], BF16)
        ones_f = sb("ones_f", [128, 128], F32)
        indicT_b = sb("indicT_b", [128, KI * 24], BF16)
        indicT64_b = sb("indicT64_b", [128, KI * 24], BF16)
        maskpos = sb("maskpos", [128, 128], F32)
        ident_f = sb("ident_f", [128, 128], F32)
        ccol = sb("ccol", [128, NCC], F32)
        v_sg = sb("v_sg", [128, KH], F32)
        v_suh = sb("v_suh", [128, KH], F32)
        v_sd = sb("v_sd", [128, KD], F32)
        v_inb = sb("v_inb", [128, 24], F32)
        v_xpb = sb("v_xpb", [128, KP], F32)
        v_opb = sb("v_opb", [128, KD], F32)
        v_cw = sb("v_cw", [128, KI * 4], F32)
        v_cb = sb("v_cb", [128, KI], F32)
        v_nw = sb("v_nw", [128, KD], F32)
        v_hd = sb("v_hd", [24, 8], F32)
        rg_col = sb("rg_col", [128, 1], F32)
        zer24 = sb("zer24", [24, CHK], F32)
        zcol = sb("zcol", [128, 8], BF16)

        pb = [ctx.enter_context(nc.psum_tensor(f"pb{i}", [128, 512], F32)) for i in range(8)]

        block = ctx.enter_context(nc.Block())
        PH = [0]

        def uname(n_):
            PH[0] += 1
            return f"{n_}_{PH[0]}"

        def phase_on(p):
            return p <= phases_upto

        @block.sync
        def _(sync):
            for t_, a_ in ((ones_f, ones_h), (maskpos, maskpos_h), (ident_f, ident_h),
                           (ccol, ccol_h), (v_sg, sgate), (v_suh, sup), (v_sd, sdown),
                           (v_inb, inb), (v_xpb, xpb), (v_opb, opb), (v_cw, convw),
                           (v_cb, convb), (v_nw, normw)):
                sLd.inc(sync.dma_start(out=t_[:], in_=a_), 16)
            sLd.inc(sync.dma_start(out=v_hd[:, 0:4], in_=headv), 16)
            sLd.inc(sync.dma_start(out=rg_col[0:1, 0:1], in_=resg), 16)

        @block.gpsimd
        def _(gp):
            for t_, a_ in ((ones_b, ones_h), (indic_b, indic_h), (indicT_b, indicT_h),
                           (indicT64_b, indicT64_h)):
                sWl.inc(gp.dma_start(out=t_[:], in_=a_), 16)
            gp.memset(zer24[:], 0.0)
            sPool.inc(gp.memset(zcol[:], 0.0))

        @block.vector
        def _(v):
            v.wait_ge(sLd.h, sLd.v)
            v.wait_ge(sWl.h, sWl.v)
            v.tensor_scalar(out=v_suh[:], in0=v_suh[:], scalar1=0.5, scalar2=None, op0=ALU.mult)
            v.tensor_scalar(out=v_hd[:, 4:5], in0=v_hd[:, 0:1], scalar1=-2.3, scalar2=-0.01,
                            op0=ALU.max, op1=ALU.min)
            v.tensor_scalar(out=v_hd[:, 5:6], in0=v_hd[:, 1:2], scalar1=-3.32, scalar2=-0.015,
                            op0=ALU.max, op1=ALU.min)
            i = v.tensor_scalar(out=v_hd[:, 6:7], in0=v_hd[:, 2:3], scalar1=-2.0, scalar2=2.0,
                                op0=ALU.max, op1=ALU.min)
            sDVE.inc(i)

        @block.scalar
        def _(sc):
            sc.wait_ge(sDVE.h, sDVE.v)
            sc.activation(out=v_hd[:, 5:6], in_=v_hd[:, 5:6], func=AF.Exp, scale=LN2)
            i = sc.activation(out=v_hd[:, 6:7], in_=v_hd[:, 6:7], func=AF.Exp, scale=LN2)
            sACT.inc(i)

        @block.tensor
        def _(te):
            te.wait_ge(sLd.h, sLd.v)
            i = te.matmul(pb[0][:, 0:1], ones_f[0:1, :], rg_col[0:1, 0:1], start=True, stop=True)
            sPE.inc(i)

        @block.vector
        def _(v):
            v.wait_ge(sACT.h, sACT.v)
            v.wait_ge(sPE.h, sPE.v)
            v.tensor_scalar(out=v_hd[:, 5:6], in0=v_hd[:, 5:6], scalar1=-1.0, scalar2=1.0,
                            op0=ALU.mult, op1=ALU.add)
            v.tensor_scalar(out=v_hd[:, 7:8], in0=v_hd[:, 5:6], scalar1=-1.0, scalar2=1.0,
                            op0=ALU.mult, op1=ALU.add)
            i = v.tensor_scalar(out=v_hd[:, 3:4], in0=v_hd[:, 3:4], scalar1=1e-6, scalar2=None,
                                op0=ALU.add)
            sDVE.inc(i)
            v.wait_ge(sDVE.h, sDVE.v)
            v.reciprocal(out=v_hd[:, 3:4], in_=v_hd[:, 3:4])
            i = v.tensor_copy(out=rg_col[:], in_=pb[0][:, 0:1])
            sDVE.inc(i)
            v.wait_ge(sDVE.h, sDVE.v)
            i = v.tensor_scalar(out=v_opb[:], in0=v_opb[:], scalar1=rg_col[:, 0:1], scalar2=None,
                                op0=ALU.mult)
            sDVE.inc(i)

        @block.sync
        def _(sync):
            sync.wait_ge(sPool.h, sPool.v)
            for c in range(KI):
                for b in range(NB):
                    i = sync.dma_start(out=xpad[128 * c:128 * (c + 1), b * XPW:b * XPW + PADL],
                                       in_=zcol[:, 0:PADL])
                    sSt.inc(i, 16)

        def emit_ln(src, rms):
            NLN = T // 512
            APN = 4 if rms else 3
            with ExitStack() as px:
                psb = lambda n_, s_, d_=F32: px.enter_context(nc.sbuf_tensor(uname(n_), s_, d_))
                resn = psb("ln_resn", [128, KD * 512], F32)
                rbt = psb("ln_rbt", [128, KD * 512], BF16)
                rsq = psb("ln_rsq", [128, KD * 512], BF16)
                mrep = psb("ln_mrep", [128, 512], F32)
                srep = psb("ln_srep", [128, 512], F32)
                lt1 = psb("ln_t1", [128, 512], F32)
                lt2 = psb("ln_t2", [128, 512], F32)

                sn0 = snap()
                ld0, pe0, act0, dve0, po0, st0 = sLd.v, sPE.v, sACT.v, sDVE.v, sPool.v, sSt.v
                scale_a = (1.0 + RMS_EPS) if rms else 1.0
                bias_cc = CC_RMSB if rms else CC_LNEPS

                @block.sync
                def _(sync):
                    bwait(sync, sn0)
                    for n in range(NLN):
                        if n >= 1:
                            sync.wait_ge(sDVE.h, dve0 + 3 * n)
                        i = sync.dma_start(
                            out=resn[:, 0:6 * 512].rearrange("p (k c) -> p k c", c=512),
                            in_=src[:, 512 * n:512 * (n + 1)].rearrange(
                                "(k p) c -> p k c", p=128)[:, 0:6])
                        sLd.inc(i, 16)
                        if rms:
                            sync.wait_ge(sDVE.h, dve0 + 3 * n + 2)
                            i = sync.dma_start(out=lnm_d[:, 512 * n:512 * (n + 1)], in_=mrep[:])
                            sSt.inc(i, 16)
                            sync.wait_ge(sSt.h, sSt.v)
                            i = sync.dma_start(out=lnr_d[:, 512 * n:512 * (n + 1)], in_=lt1[:])
                            sSt.inc(i, 16)
                            sync.wait_ge(sSt.h, sSt.v)

                @block.scalar
                def _(sc):
                    bwait(sc, sn0)
                    wl0, sc0 = sWl.v, sSc.v
                    for n in range(NLN):
                        if n >= 1:
                            sc.wait_ge(sDVE.h, dve0 + 3 * n)
                        i = sc.dma_start(
                            out=resn[:, 11 * 512:16 * 512].rearrange("p (k c) -> p k c", c=512),
                            in_=src[:, 512 * n:512 * (n + 1)].rearrange(
                                "(k p) c -> p k c", p=128)[:, 11:16])
                        sSc.inc(i, 16)
                        sc.wait_ge(sLd.h, ld0 + 16 * (n + 1))
                        sc.wait_ge(sWl.h, wl0 + 16 * (n + 1))
                        sc.wait_ge(sSc.h, sc0 + 16 * (n + 1))
                        sACT.inc(sc.copy(out=rbt[:], in_=resn[:]))
                        sACT.inc(sc.activation(out=rsq[:], in_=resn[:], func=AF.Square))
                        sc.wait_ge(sDVE.h, dve0 + 3 * n + 1)
                        i = sc.activation(out=lt2[:], in_=lt1[:], func=AF.Sqrt,
                                          scale=scale_a, bias=ccol[:, bias_cc:bias_cc + 1])
                        sACT.inc(i)
                        if rms:
                            i = sc.activation(out=srep[:], in_=lt1[:], func=AF.Sqrt,
                                              bias=ccol[:, CC_LNEPS:CC_LNEPS + 1])
                            sACT.inc(i)

                @block.tensor
                def _(te):
                    bwait(te, sn0)
                    for n in range(NLN):
                        pS1 = pb[4] if n % 2 == 0 else pb[6]
                        pS2 = pb[5] if n % 2 == 0 else pb[7]
                        te.wait_ge(sACT.h, act0 + APN * n + 2)
                        if n >= 2:
                            te.wait_ge(sDVE.h, dve0 + 3 * (n - 1) + 1)
                        for k in range(KD):
                            te.matmul(pS1[:], ones_b[:], rbt[:, 512 * k:512 * (k + 1)],
                                      start=(k == 0), stop=(k == KD - 1))
                            i = te.matmul(pS2[:], ones_b[:], rsq[:, 512 * k:512 * (k + 1)],
                                          start=(k == 0), stop=(k == KD - 1))
                        sPE.inc(i)

                @block.vector
                def _(v):
                    bwait(v, sn0)
                    for n in range(NLN):
                        pS1 = pb[4] if n % 2 == 0 else pb[6]
                        pS2 = pb[5] if n % 2 == 0 else pb[7]
                        v.wait_ge(sPE.h, pe0 + n + 1)
                        if rms and n >= 1:
                            v.wait_ge(sSt.h, st0 + 32 * n)
                        v.tensor_scalar(out=mrep[:], in0=pS1[:], scalar1=1.0 / D, scalar2=None,
                                        op0=ALU.mult)
                        v.tensor_scalar(out=lt1[:], in0=pS2[:], scalar1=1.0 / D, scalar2=None,
                                        op0=ALU.mult)
                        v.tensor_tensor(out=lt2[:], in0=mrep[:], in1=mrep[:], op=ALU.mult)
                        i = v.tensor_tensor(out=lt1[:], in0=lt1[:], in1=lt2[:], op=ALU.subtract)
                        sDVE.inc(i)
                        v.wait_ge(sACT.h, act0 + APN * (n + 1))
                        if rms:
                            v.reciprocal(out=lt1[:], in_=srep[:])
                            i = v.reciprocal(out=srep[:], in_=lt2[:])
                        else:
                            i = v.reciprocal(out=srep[:], in_=lt2[:])
                        sDVE.inc(i)
                        v.wait_ge(sDVE.h, sDVE.v)
                        v.wait_ge(sPool.h, po0 + n + 1)
                        for k in range(KD):
                            dst = xln[:, 4096 * k + 512 * n:4096 * k + 512 * (n + 1)]
                            if rms:
                                i = v.scalar_tensor_tensor(
                                    out=dst, in0=dst, scalar=v_nw[:, k:k + 1], in1=srep[:],
                                    op0=ALU.mult, op1=ALU.mult)
                            else:
                                i = v.tensor_tensor(out=dst, in0=dst, in1=srep[:], op=ALU.mult)
                        sDVE.inc(i)

                @block.gpsimd
                def _(gp):
                    bwait(gp, sn0)
                    for n in range(NLN):
                        if n >= 1:
                            gp.wait_ge(sDVE.h, dve0 + 3 * n)
                        i = gp.dma_start(
                            out=resn[:, 6 * 512:11 * 512].rearrange("p (k c) -> p k c", c=512),
                            in_=src[:, 512 * n:512 * (n + 1)].rearrange(
                                "(k p) c -> p k c", p=128)[:, 6:11])
                        sWl.inc(i, 16)
                        gp.wait_ge(sDVE.h, dve0 + 3 * n + 1)
                        for k in range(KD):
                            i = gp.tensor_tensor(
                                out=xln[:, 4096 * k + 512 * n:4096 * k + 512 * (n + 1)],
                                in0=resn[:, 512 * k:512 * (k + 1)],
                                in1=mrep[:], op=ALU.subtract)
                        sPool.inc(i)

        def emit_gateup(wdprep):
            with ExitStack() as px:
                psb = lambda n_, s_, d_=F32: px.enter_context(nc.sbuf_tensor(uname(n_), s_, d_))
                wgi = psb("gu_wgi", [128, 2 * 2048], BF16)
                wui = psb("gu_wui", [128, 2 * 2048], BF16)
                wgb = psb("gu_wgb", [128, 2 * 2048], BF16)
                wub = psb("gu_wub", [128, 2 * 2048], BF16)
                sgt = psb("gu_sgt", [128, 2 * 1024], BF16)
                gt = psb("gu_gt", [128, 2 * 1024], BF16)
                wdi = psb("gu_wdi", [128, 2 * 2048], BF16) if wdprep else None
                wdb = psb("gu_wdb", [128, 2 * 2048], BF16) if wdprep else None

                sn0 = snap()
                wl0, pe0, act0, dve0, st0, sc0 = sWl.v, sPE.v, sACT.v, sDVE.v, sSt.v, sSc.v
                APM = 7 if wdprep else 6
                WPM = 48 if wdprep else 32

                @block.gpsimd
                def _(gp):
                    bwait(gp, sn0)
                    for m in range(KH):
                        s = (m % 2)
                        if m >= 2:
                            gp.wait_ge(sACT.h, act0 + APM * (m - 1) + 2 + (1 if wdprep else 0))
                        i = gp.dma_start(
                            out=wgi[:, 2048 * s:2048 * (s + 1)].rearrange("p (k c) -> p k c", c=128),
                            in_=wgT[:, 128 * m:128 * (m + 1)].rearrange("(k p) c -> p k c", p=128))
                        sWl.inc(i, 16)
                        i = gp.dma_start(
                            out=wui[:, 2048 * s:2048 * (s + 1)].rearrange("p (k c) -> p k c", c=128),
                            in_=wuT[:, 128 * m:128 * (m + 1)].rearrange("(k p) c -> p k c", p=128))
                        sWl.inc(i, 16)
                        if wdprep:
                            i = gp.dma_start(out=wdi[:, 2048 * s:2048 * (s + 1)],
                                             in_=wdT[128 * m:128 * (m + 1), :])
                            sWl.inc(i, 16)

                @block.scalar
                def _(sc):
                    bwait(sc, sn0)
                    for m in range(KH):
                        s = (m % 2)
                        sc.wait_ge(sWl.h, wl0 + WPM * (m + 1))
                        sACT.inc(sc.activation(out=wgb[:, 2048 * s:2048 * (s + 1)],
                                               in_=wgi[:, 2048 * s:2048 * (s + 1)],
                                               func=AF.Copy, bias=-128.0))
                        sACT.inc(sc.activation(out=wub[:, 2048 * s:2048 * (s + 1)],
                                               in_=wui[:, 2048 * s:2048 * (s + 1)],
                                               func=AF.Copy, bias=-128.0))
                        if wdprep:
                            if m >= 2:
                                sc.wait_ge(sSc.h, sc0 + 256 * m)
                            sACT.inc(sc.activation(out=wdb[:, 2048 * s:2048 * (s + 1)],
                                                   in_=wdi[:, 2048 * s:2048 * (s + 1)],
                                                   func=AF.Copy, bias=-128.0))
                            sc.wait_ge(sACT.h, sACT.v)
                            for md in range(KD):
                                i = sc.dma_start(
                                    out=wd_bf2[md, 128 * m:128 * (m + 1), :],
                                    in_=wdb[:, 2048 * s + 128 * md:2048 * s + 128 * (md + 1)])
                                sSc.inc(i, 16)
                        for q in range(4):
                            qi = 4 * m + q
                            sc.wait_ge(sPE.h, pe0 + qi + 1)
                            if qi >= 2:
                                sc.wait_ge(sDVE.h, dve0 + qi - 1)
                            p = qi % 2
                            pg0, pg1 = pb[0 + 4 * p], pb[1 + 4 * p]
                            sc.activation(out=sgt[:, 1024 * (qi % 2):1024 * (qi % 2) + 512],
                                          in_=pg0[:], func=AF.Silu, scale=v_sg[:, m:m + 1])
                            i = sc.activation(out=sgt[:, 1024 * (qi % 2) + 512:1024 * (qi % 2 + 1)],
                                              in_=pg1[:], func=AF.Silu, scale=v_sg[:, m:m + 1])
                            sACT.inc(i)

                @block.tensor
                def _(te):
                    bwait(te, sn0)
                    for m in range(KH):
                        s = (m % 2)
                        te.wait_ge(sACT.h, act0 + APM * m + 2)
                        for q in range(4):
                            qi = 4 * m + q
                            if qi >= 2:
                                te.wait_ge(sDVE.h, dve0 + qi - 1)
                            p = qi % 2
                            pg0, pg1 = pb[0 + 4 * p], pb[1 + 4 * p]
                            pu0, pu1 = pb[2 + 4 * p], pb[3 + 4 * p]
                            for k in range(KD):
                                wg_k = wgb[:, 2048 * s + 128 * k:2048 * s + 128 * (k + 1)]
                                wu_k = wub[:, 2048 * s + 128 * k:2048 * s + 128 * (k + 1)]
                                x0 = xln[:, 4096 * k + 1024 * q:4096 * k + 1024 * q + 512]
                                x1 = xln[:, 4096 * k + 1024 * q + 512:4096 * k + 1024 * (q + 1)]
                                te.matmul(pg0[:], wg_k, x0, start=(k == 0), stop=(k == KD - 1))
                                te.matmul(pg1[:], wg_k, x1, start=(k == 0), stop=(k == KD - 1))
                                te.matmul(pu0[:], wu_k, x0, start=(k == 0), stop=(k == KD - 1))
                                i = te.matmul(pu1[:], wu_k, x1, start=(k == 0), stop=(k == KD - 1))
                            sPE.inc(i)

                @block.vector
                def _(v):
                    bwait(v, sn0)
                    for m in range(KH):
                        for q in range(4):
                            qi = 4 * m + q
                            v.wait_ge(sACT.h, act0 + APM * m + 2 + (1 if wdprep else 0) + q + 1)
                            v.wait_ge(sPE.h, pe0 + qi + 1)
                            if qi >= 2:
                                v.wait_ge(sSt.h, st0 + 16 * (qi - 1))
                            p = qi % 2
                            pu0, pu1 = pb[2 + 4 * p], pb[3 + 4 * p]
                            gs = 1024 * (qi % 2)
                            v.scalar_tensor_tensor(out=gt[:, gs:gs + 512], in0=pu0[:],
                                                   scalar=v_suh[:, m:m + 1],
                                                   in1=sgt[:, gs:gs + 512],
                                                   op0=ALU.mult, op1=ALU.mult)
                            i = v.scalar_tensor_tensor(out=gt[:, gs + 512:gs + 1024], in0=pu1[:],
                                                       scalar=v_suh[:, m:m + 1],
                                                       in1=sgt[:, gs + 512:gs + 1024],
                                                       op0=ALU.mult, op1=ALU.mult)
                            sDVE.inc(i)

                @block.sync
                def _(sync):
                    bwait(sync, sn0)
                    for m in range(KH):
                        for q in range(4):
                            qi = 4 * m + q
                            sync.wait_ge(sDVE.h, dve0 + qi + 1)
                            gs = 1024 * (qi % 2)
                            i = sync.dma_start(out=g_bf[128 * m:128 * (m + 1),
                                                        1024 * q:1024 * (q + 1)],
                                               in_=gt[:, gs:gs + 1024])
                            sSt.inc(i, 16)
                            sync.wait_ge(sSt.h, sSt.v)

        def emit_down(srcres, dstres):
            with ExitStack() as px:
                psb = lambda n_, s_, d_=F32: px.enter_context(nc.sbuf_tensor(uname(n_), s_, d_))
                wdb = psb("dn_wdb", [128, 2 * KH * 128], BF16)
                resd = psb("dn_resd", [128, 2 * 1024], F32)
                nres = psb("dn_nres", [128, 2 * 1024], F32)

                sn0 = snap()
                ld0, pe0, dve0, st0 = sLd.v, sPE.v, sDVE.v, sSt.v
                q0 = [sQ[0].v, sQ[1].v]
                r0 = [sR[0].v, sR[1].v]

                @block.sync
                def _(sync):
                    bwait(sync, sn0)
                    for q in range(4):
                        if q >= 1:
                            sync.wait_ge(sPE.h, pe0 + 16 * q)
                        for k0, k1 in ((0, 11), (11, 22), (22, 33), (33, 44)):
                            i = sync.dma_start(
                                out=xln[:, 1024 * k0:1024 * k1].rearrange(
                                    "p (k c) -> p k c", c=1024),
                                in_=g_bf[128 * k0:128 * k1,
                                         1024 * q:1024 * (q + 1)].rearrange(
                                    "(k p) c -> p k c", p=128))
                            sLd.inc(i, 16)
                        for md in range(KD):
                            t = 16 * q + md
                            sync.wait_ge(sDVE.h, dve0 + t + 1)
                            i = sync.dma_start(out=dstres[128 * md:128 * (md + 1),
                                                          1024 * q:1024 * (q + 1)],
                                               in_=nres[:, 1024 * (t % 2):1024 * (t % 2 + 1)])
                            sSt.inc(i, 16)
                            sync.wait_ge(sSt.h, sSt.v)

                @block.scalar
                def _(sc):
                    bwait(sc, sn0)
                    for q in range(4):
                        for md in range(KD):
                            t = 16 * q + md
                            s = md % 2
                            if t >= 2:
                                sc.wait_ge(sPE.h, pe0 + t - 1)
                            i = sc.dma_start(
                                out=wdb[:, KH * 128 * s:KH * 128 * (s + 1)].rearrange(
                                    "p (k c) -> p k c", c=128),
                                in_=wd_bf2[md].rearrange("(k p) c -> p k c", p=128))
                            sQ[s].inc(i, 16)
                            if t >= 2:
                                sc.wait_ge(sDVE.h, dve0 + t - 1)
                            i = sc.dma_start(out=resd[:, 1024 * s:1024 * (s + 1)],
                                             in_=srcres[128 * md:128 * (md + 1),
                                                        1024 * q:1024 * (q + 1)])
                            sR[s].inc(i, 16)

                @block.tensor
                def _(te):
                    bwait(te, sn0)
                    for q in range(4):
                        for md in range(KD):
                            t = 16 * q + md
                            s = md % 2
                            te.wait_ge(sQ[s].h, q0[s] + 16 * (t // 2 + 1))
                            if md == 0:
                                te.wait_ge(sLd.h, ld0 + 64 * (q + 1))
                            if t >= 2:
                                te.wait_ge(sDVE.h, dve0 + t - 1)
                            pA = pb[0 + 2 * s]
                            pB = pb[1 + 2 * s]
                            for k in range(KH):
                                wk = wdb[:, KH * 128 * s + 128 * k:KH * 128 * s + 128 * (k + 1)]
                                te.matmul(pA[:], wk, xln[:, 1024 * k:1024 * k + 512],
                                          start=(k == 0), stop=(k == KH - 1))
                                i = te.matmul(pB[:], wk, xln[:, 1024 * k + 512:1024 * (k + 1)],
                                              start=(k == 0), stop=(k == KH - 1))
                            sPE.inc(i)

                @block.vector
                def _(v):
                    bwait(v, sn0)
                    for q in range(4):
                        for md in range(KD):
                            t = 16 * q + md
                            s = md % 2
                            v.wait_ge(sPE.h, pe0 + t + 1)
                            v.wait_ge(sR[s].h, r0[s] + 16 * (t // 2 + 1))
                            if t >= 2:
                                v.wait_ge(sSt.h, st0 + 16 * (t - 1))
                            pA = pb[0 + 2 * s]
                            pB = pb[1 + 2 * s]
                            v.scalar_tensor_tensor(out=nres[:, 1024 * s:1024 * s + 512],
                                                   in0=pA[:], scalar=v_sd[:, md:md + 1],
                                                   in1=resd[:, 1024 * s:1024 * s + 512],
                                                   op0=ALU.mult, op1=ALU.add)
                            i = v.scalar_tensor_tensor(out=nres[:, 1024 * s + 512:1024 * (s + 1)],
                                                       in0=pB[:], scalar=v_sd[:, md:md + 1],
                                                       in1=resd[:, 1024 * s + 512:1024 * (s + 1)],
                                                       op0=ALU.mult, op1=ALU.add)
                            sDVE.inc(i)

        def emit_proj(wdram, KC, MC, kind, srcres=None, dstres=None):
            with ExitStack() as px:
                psb = lambda n_, s_, d_=F32: px.enter_context(nc.sbuf_tensor(uname(n_), s_, d_))
                wib = psb("pj_wib", [128, 2 * KC * 128], BF16)
                gt = psb("pj_gt", [128, 2 * 1024], BF16)
                if kind == "out":
                    resd = psb("pj_resd", [128, 2 * 1024], F32)
                    nres = psb("pj_nres", [128, 2 * 1024], F32)
                    lnm_s = psb("pj_lnm", [128, T], F32)
                    lnr_s = psb("pj_lnr", [128, T], F32)
                    u1 = psb("pj_u1", [128, 1024], F32)
                    u3 = psb("pj_u3", [128, 1024], F32)

                sn0 = snap()
                wl0, pe0, act0, dve0, st0, sc0 = sWl.v, sPE.v, sACT.v, sDVE.v, sSt.v, sSc.v
                ld0 = sLd.v
                q0 = [sQ[0].v, sQ[1].v]
                r0 = [sR[0].v, sR[1].v]

                @block.gpsimd
                def _(gp):
                    bwait(gp, sn0)
                    for m in range(MC):
                        s = m % 2
                        if m >= 2:
                            gp.wait_ge(sPE.h, pe0 + 4 * (m - 1))
                        i = gp.dma_start(
                            out=wib[:, KC * 128 * s:KC * 128 * (s + 1)].rearrange(
                                "p (k c) -> p k c", c=128),
                            in_=wdram[:, 128 * m:128 * (m + 1)].rearrange("(k p) c -> p k c", p=128))
                        sQ[s].inc(i, 16)

                @block.scalar
                def _(sc):
                    bwait(sc, sn0)
                    for m in range(MC):
                        if kind == "out":
                            for q in range(4):
                                t = 4 * m + q
                                s = t % 2
                                if t >= 2:
                                    sc.wait_ge(sDVE.h, dve0 + t - 1)
                                i = sc.dma_start(out=resd[:, 1024 * s:1024 * (s + 1)],
                                                 in_=srcres[128 * m:128 * (m + 1),
                                                            1024 * q:1024 * (q + 1)])
                                sR[s].inc(i, 16)
                        if kind == "in" and m >= 12:
                            for q in range(4):
                                t = 4 * m + q
                                sc.wait_ge(sPE.h, pe0 + t + 1)
                                sc.wait_ge(sDVE.h, dve0 + t)
                                if t >= 2:
                                    sc.wait_ge(sSt.h, st0 + 16 * (t - 1))
                                p = t % 2
                                pA, pB = pb[0 + 2 * p], pb[1 + 2 * p]
                                gs = 1024 * (t % 2)
                                zb = v_inb[:, m:m + 1]
                                sc.activation(out=gt[:, gs:gs + 512], in_=pA[:], func=AF.Silu,
                                              bias=zb)
                                i = sc.activation(out=gt[:, gs + 512:gs + 1024], in_=pB[:],
                                                  func=AF.Silu, bias=zb)
                                sDVE.inc(i)

                @block.tensor
                def _(te):
                    bwait(te, sn0)
                    for m in range(MC):
                        s = m % 2
                        te.wait_ge(sQ[s].h, q0[s] + 16 * (m // 2 + 1))
                        for q in range(4):
                            t = 4 * m + q
                            if t >= 2:
                                te.wait_ge(sDVE.h, dve0 + t - 1)
                            p = t % 2
                            pA, pB = pb[0 + 2 * p], pb[1 + 2 * p]
                            for k in range(KC):
                                wk = wib[:, KC * 128 * s + 128 * k:KC * 128 * s + 128 * (k + 1)]
                                x0 = xln[:, 4096 * k + 1024 * q:4096 * k + 1024 * q + 512]
                                x1 = xln[:, 4096 * k + 1024 * q + 512:4096 * k + 1024 * (q + 1)]
                                te.matmul(pA[:], wk, x0, start=(k == 0), stop=(k == KC - 1))
                                i = te.matmul(pB[:], wk, x1, start=(k == 0), stop=(k == KC - 1))
                            sPE.inc(i)

                @block.vector
                def _(v):
                    bwait(v, sn0)
                    for m in range(MC):
                        for q in range(4):
                            t = 4 * m + q
                            if kind == "in" and m >= 12:
                                continue
                            v.wait_ge(sPE.h, pe0 + t + 1)
                            if t >= 2:
                                v.wait_ge(sSt.h, st0 + 16 * (t - 1))
                            p = t % 2
                            pA, pB = pb[0 + 2 * p], pb[1 + 2 * p]
                            gs = 1024 * (t % 2)
                            if kind == "in":
                                bcol = v_inb[:, m:m + 1]
                                v.tensor_scalar(out=gt[:, gs:gs + 512], in0=pA[:], scalar1=bcol,
                                                scalar2=None, op0=ALU.add)
                                i = v.tensor_scalar(out=gt[:, gs + 512:gs + 1024], in0=pB[:],
                                                    scalar1=bcol, scalar2=None, op0=ALU.add)
                                sDVE.inc(i)
                            elif kind == "xp":
                                bcol = v_xpb[:, m:m + 1]
                                v.tensor_scalar(out=gt[:, gs:gs + 512], in0=pA[:], scalar1=bcol,
                                                scalar2=None, op0=ALU.add)
                                i = v.tensor_scalar(out=gt[:, gs + 512:gs + 1024], in0=pB[:],
                                                    scalar1=bcol, scalar2=None, op0=ALU.add)
                                sDVE.inc(i)
                            else:
                                v.wait_ge(sR[t % 2].h, r0[t % 2] + 16 * (t // 2 + 1))
                                if t == 0:
                                    v.wait_ge(sLd.h, ld0 + 32)
                                ns = 1024 * (t % 2)
                                lm = lnm_s[:, 1024 * q:1024 * (q + 1)]
                                lr = lnr_s[:, 1024 * q:1024 * (q + 1)]
                                v.tensor_tensor(out=u1[:], in0=resd[:, ns:ns + 1024],
                                                in1=lm, op=ALU.subtract)
                                v.tensor_tensor(out=u1[:], in0=u1[:], in1=lr, op=ALU.mult)
                                v.tensor_scalar(out=u3[:, 0:512], in0=pA[:],
                                                scalar1=rg_col[:, 0:1],
                                                scalar2=v_opb[:, m:m + 1],
                                                op0=ALU.mult, op1=ALU.add)
                                v.tensor_scalar(out=u3[:, 512:1024], in0=pB[:],
                                                scalar1=rg_col[:, 0:1],
                                                scalar2=v_opb[:, m:m + 1],
                                                op0=ALU.mult, op1=ALU.add)
                                v.tensor_tensor(out=u1[:], in0=u1[:], in1=u3[:], op=ALU.add)
                                i = v.tensor_tensor(out=nres[:, ns:ns + 1024], in0=u1[:],
                                                    in1=resd[:, ns:ns + 1024], op=ALU.add)
                                sDVE.inc(i)

                @block.sync
                def _(sync):
                    bwait(sync, sn0)
                    if kind == "out":
                        sLd.inc(sync.dma_start(out=lnm_s[:], in_=lnm_d), 16)
                        sLd.inc(sync.dma_start(out=lnr_s[:], in_=lnr_d), 16)
                    for m in range(MC):
                        for q in range(4):
                            t = 4 * m + q
                            sync.wait_ge(sDVE.h, dve0 + t + 1)
                            gs = 1024 * (t % 2)
                            if kind == "in":
                                if m < 12:
                                    b = q // 2
                                    col = b * XPW + PADL + 1024 * (q % 2)
                                    dst = xpad[128 * m:128 * (m + 1), col:col + 1024]
                                    i = sync.dma_start(out=dst, in_=gt[:, gs:gs + 1024])
                                else:
                                    dst = siluzT[128 * (m - 12):128 * (m - 11),
                                                 1024 * q:1024 * (q + 1)]
                                    i = sync.dma_start(out=dst, in_=gt[:, gs:gs + 1024])
                            elif kind == "xp":
                                dst = xpT_s[128 * m:128 * (m + 1), 1024 * q:1024 * (q + 1)]
                                i = sync.dma_start(out=dst, in_=gt[:, gs:gs + 1024])
                            else:
                                ns = 1024 * (t % 2)
                                dst = dstres[128 * m:128 * (m + 1), 1024 * q:1024 * (q + 1)]
                                i = sync.dma_start(out=dst, in_=nres[:, ns:ns + 1024])
                            sSt.inc(i, 16)
                            sync.wait_ge(sSt.h, sSt.v)

        def emit_conv():
            with ExitStack() as px:
                psb = lambda n_, s_, d_=F32: px.enter_context(nc.sbuf_tensor(uname(n_), s_, d_))
                xpt = psb("cv_x", [128, 2 * XPW], BF16)
                a1 = psb("cv_a1", [128, 2 * 2048], F32)
                a2 = psb("cv_a2", [128, 2 * 2048], F32)

                sn0 = snap()
                ld0, dve0, po0, act0 = sLd.v, sDVE.v, sPool.v, sACT.v
                q0 = [sQ[0].v, sQ[1].v]

                @block.sync
                def _(sync):
                    bwait(sync, sn0)
                    for c in range(KI):
                        for b in range(NB):
                            i = 2 * c + b
                            s = i % 2
                            if i >= 2:
                                sync.wait_ge(sDVE.h, dve0 + 2 * (i - 1))
                                sync.wait_ge(sPool.h, po0 + i - 1)
                            ii = sync.dma_start(out=xpt[:, XPW * s:XPW * (s + 1)],
                                                in_=xpad[128 * c:128 * (c + 1),
                                                         b * XPW:(b + 1) * XPW])
                            sQ[s].inc(ii, 16)

                @block.vector
                def _(v):
                    bwait(v, sn0)
                    for c in range(KI):
                        for b in range(NB):
                            i = 2 * c + b
                            s = i % 2
                            v.wait_ge(sQ[s].h, q0[s] + 16 * (i // 2 + 1))
                            if i >= 2:
                                v.wait_ge(sACT.h, act0 + i - 1)
                                v.wait_ge(sPool.h, po0 + i - 1)
                            xs = xpt[:, XPW * s:XPW * (s + 1)]
                            v.tensor_scalar(out=a1[:, 2048 * s:2048 * (s + 1)],
                                            in0=xs[:, 0:2048], scalar1=v_cw[:, 4 * c:4 * c + 1],
                                            scalar2=None, op0=ALU.mult)
                            ii = v.scalar_tensor_tensor(out=a1[:, 2048 * s:2048 * (s + 1)],
                                                        in0=xs[:, 1:2049],
                                                        scalar=v_cw[:, 4 * c + 1:4 * c + 2],
                                                        in1=a1[:, 2048 * s:2048 * (s + 1)],
                                                        op0=ALU.mult, op1=ALU.add)
                            sDVE.inc(ii)
                            v.tensor_scalar(out=a2[:, 2048 * s:2048 * (s + 1)],
                                            in0=xs[:, 2:2050], scalar1=v_cw[:, 4 * c + 2:4 * c + 3],
                                            scalar2=None, op0=ALU.mult)
                            ii = v.scalar_tensor_tensor(out=a2[:, 2048 * s:2048 * (s + 1)],
                                                        in0=xs[:, 3:2051],
                                                        scalar=v_cw[:, 4 * c + 3:4 * c + 4],
                                                        in1=a2[:, 2048 * s:2048 * (s + 1)],
                                                        op0=ALU.mult, op1=ALU.add)
                            sDVE.inc(ii)

                @block.gpsimd
                def _(gp):
                    bwait(gp, sn0)
                    for c in range(KI):
                        for b in range(NB):
                            i = 2 * c + b
                            s = i % 2
                            gp.wait_ge(sDVE.h, dve0 + 2 * (i + 1))
                            ii = gp.tensor_tensor(out=a2[:, 2048 * s:2048 * (s + 1)],
                                                  in0=a1[:, 2048 * s:2048 * (s + 1)],
                                                  in1=a2[:, 2048 * s:2048 * (s + 1)], op=ALU.add)
                            sPool.inc(ii)

                @block.scalar
                def _(sc):
                    bwait(sc, sn0)
                    for c in range(KI):
                        for b in range(NB):
                            i = 2 * c + b
                            s = i % 2
                            sc.wait_ge(sPool.h, po0 + i + 1)
                            dst = xln[:, 4096 * c + 2048 * b:4096 * c + 2048 * (b + 1)]
                            ii = sc.activation(out=dst, in_=a2[:, 2048 * s:2048 * (s + 1)],
                                               func=AF.Silu, bias=v_cb[:, c:c + 1])
                            sACT.inc(ii)

        def emit_scan(np_cap=None, j_cap=None):
            NP = np_cap if np_cap is not None else NCH // 2
            JC = j_cap if j_cap is not None else 48
            KC = JC // 2
            PE_SA_U, ACT_SA_U, DVE_SA_U, PO_SA_U, LD_SA_U = 28, 23, 16, 3, 144
            PE_PP = 2 * PE_SA_U + 48 * 4
            ACT_PP = 2 * ACT_SA_U + 24 * 7
            DVE_PP = 2 * DVE_SA_U + 24 * 12
            PO_PP = 2 * PO_SA_U + 48 * 2
            LD_PP = 2 * LD_SA_U

            def PEj(j, e):
                k, u = j // 2, j % 2
                if e == 0:
                    return 2 * PE_SA_U + 8 * k + 1 + u
                if e == 1:
                    return 2 * PE_SA_U + 8 * k + 3 + u
                return 2 * PE_SA_U + 8 * k + 3 + 2 * u + e

            def DVEj(j, e):
                k, u = j // 2, j % 2
                if e == 0:
                    return 2 * DVE_SA_U + 12 * k + 1 + u
                return 2 * DVE_SA_U + 12 * k + 2 + 5 * u + e

            def ACTrt(j):
                return 2 * ACT_SA_U + 7 * (j // 2) + 1 + (j % 2)

            def ACTdexp(k):
                return 2 * ACT_SA_U + 7 * k + 3

            def ACTw2c(j):
                return 2 * ACT_SA_U + 7 * (j // 2) + (4 if j % 2 == 0 else 6)

            def ACTdco(j):
                return 2 * ACT_SA_U + 7 * (j // 2) + (5 if j % 2 == 0 else 7)

            def POj(j, e):
                return 2 * PO_SA_U + 2 * j + e + 1

            TB = 49152

            with ExitStack() as px:
                psb = lambda n_, s_, d_=F32: px.enter_context(nc.sbuf_tensor(uname(n_), s_, d_))
                xpc = psb("sn_xpc", [128, 2 * P3R], BF16)
                szc = psb("sn_szc", [128, 2 * DI], BF16)
                sqt = psb("sn_sqt", [128, 3072], BF16)
                cstat = psb("sn_cstat", [24, 384], F32)
                ntmp = psb("sn_ntmp", [24, 128], F32)
                nrm = psb("sn_nrm", [24, 256], F32)
                rr = psb("sn_rr", [24, 256], F32)
                lnrc = psb("sn_lnrc", [24, 128], F32)
                dta = psb("sn_dta", [24, 128], F32)
                dtb = psb("sn_dtb", [24, 128], F32)
                ltl = psb("sn_ltl", [24, 128], F32)
                att_t = psb("sn_at", [24, 128], F32)
                adec = psb("sn_adec", [24, 128], F32)
                oma_b = psb("sn_omab", [24, 128], BF16)
                pet = psb("sn_pet", [24, 128], F32)
                pt1 = psb("sn_pt1", [24, 128], F32)
                pt2 = psb("sn_pt2", [24, 128], F32)
                b1t = psb("sn_b1", [24, 128], F32)
                tht = psb("sn_th", [24, 128], F32)
                alf = psb("sn_alf", [24, 128], F32)
                omal_b = psb("sn_omal", [24, 128], BF16)
                lat = psb("sn_la", [24, 128], F32)
                cum24 = psb("sn_cum", [24, 128], F32)
                cumP = psb("sn_cumP", [24, 128], F32)
                ecr = psb("sn_ecr", [24, 128], F32)
                st48 = psb("sn_st48", [64, 128], F32)
                ctm = psb("sn_ctm", [128, 2 * 64], F32)
                qr48 = psb("sn_qr48", [64, 128], BF16)
                qrowS = psb("sn_qrow", [1, 3072], BF16)
                xbs0 = psb("sn_xbs0", [128, 3072], BF16)
                h0 = psb("sn_h0", [64, 2 * 1536], BF16)
                rtp = psb("sn_rtp", [128, 2 * 128], F32)
                dxp = psb("sn_dxp", [128, 2 * 128], BF16)
                wtb = psb("sn_wtb", [128, 2 * 128], BF16)
                smb = psb("sn_smb", [128, 2 * 128], BF16)
                btl = psb("sn_btl", [128, 2 * 64], BF16)
                ddt = psb("sn_ddt", [64, 2 * 64], BF16)
                cn = psb("sn_cn", [128, 2 * 128], BF16)
                ctl = psb("sn_ctl", [64, 2 * 128], BF16)
                w2c = psb("sn_w2c", [128, 2], F32)
                dco = psb("sn_dco", [128, 2], F32)

                def tailR(u):
                    return xln[:, TB + 3072 * u:TB + 3072 * (u + 1)]

                def tailE(u):
                    return xln[:, TB + 6144 + 3072 * u:TB + 6144 + 3072 * (u + 1)]

                def xbs(u):
                    return xbs0[:] if u == 0 else xln[:, TB + 12288:TB + 15360]

                sn0 = snap()
                ld0, pe0, act0, dve0, po0 = sLd.v, sPE.v, sACT.v, sDVE.v, sPool.v
                q0 = [sQ[0].v, sQ[1].v]

                M_PE = lambda pp, idx: pe0 + pp * PE_PP + idx
                M_ACT = lambda pp, idx: act0 + pp * ACT_PP + idx
                M_DVE = lambda pp, idx: dve0 + pp * DVE_PP + idx
                M_PO = lambda pp, idx: po0 + pp * PO_PP + idx
                M_LD = lambda pp, idx: ld0 + pp * LD_PP + idx
                M_Q = lambda u, pp, h: q0[u] + 384 * pp + 16 * (h + 1)

                def colof(pp, u):
                    return 2048 * u + 128 * pp

                def pRv(u):
                    return pb[4 + u][:, 0:128]

                def pGv(u):
                    return pb[6 + u][:, 128:256]

                def pYv(u, hb):
                    return pb[6 + u][hb:hb + 64, 256:384]

                def pHv(u):
                    return pb[4 + u][0:64, 384:448]

                @block.sync
                def _(sync):
                    bwait(sync, sn0)
                    for pp in range(NP):
                        for u in range(2):
                            col0 = colof(pp, u)
                            if pp >= 1:
                                sync.wait_ge(sPE.h, M_PE(pp - 1, PEj(46 + u, 1)))
                                sync.wait_ge(sPool.h, M_PO(pp - 1, POj(46 + u, 1)))
                            src = xpT_s[:, col0:col0 + 128]
                            for j0, j1 in ((0, 7), (7, 13), (13, 19), (19, 25)):
                                ii = sync.dma_start(
                                    out=xpc[:, P3R * u + j0 * 128:P3R * u + j1 * 128].rearrange(
                                        "p (j c) -> p j c", c=128),
                                    in_=src.rearrange("(j p) c -> p j c", p=128)[:, j0:j1])
                                sLd.inc(ii, 16)
                            if pp >= 1:
                                sync.wait_ge(sDVE.h, M_DVE(pp - 1, DVEj(46 + u, 5)))
                            for j0, j1 in ((0, 6), (6, 12)):
                                ii = sync.dma_start(
                                    out=szc[:, DI * u + j0 * 128:DI * u + j1 * 128].rearrange(
                                        "p (j c) -> p j c", c=128),
                                    in_=siluzT[:, col0:col0 + 128].rearrange(
                                        "(j p) c -> p j c", p=128)[:, j0:j1])
                                sLd.inc(ii, 16)
                            sync.wait_ge(sPool.h, M_PO(pp, PO_SA_U * u + 3))
                            ii = sync.dma_start(out=qrow_d, in_=qr48[:])
                            sLd.inc(ii, 16)
                            sync.wait_ge(sLd.h, sLd.v)
                            ri0 = 2 * u
                            for s, r0, r1 in ((0, 0, 24), (1, 32, 56)):
                                ri = ri0 + s
                                if ri >= 1:
                                    up, sp = divmod(ri - 1, 2)
                                    sync.wait_ge(sPE.h,
                                                 M_PE(pp, PE_SA_U * up + 16 + 6 * (sp + 1)))
                                elif pp >= 1:
                                    sync.wait_ge(sPE.h, M_PE(pp - 1, 2 * PE_SA_U))
                                ii = sync.dma_start(
                                    out=qrowS[:],
                                    in_=qrow_d[r0:r1].rearrange("a b -> (a b)")[None, :])
                                sLd.inc(ii, 16)
                        for k in range(KC):
                            j = 2 * k
                            if k == 0:
                                sync.wait_ge(sDVE.h, M_DVE(pp, 16))
                                if pp >= 1:
                                    sync.wait_ge(sPE.h, M_PE(pp - 1, PEj(46, 3)))
                                    sync.wait_ge(sDVE.h, M_DVE(pp - 1, DVEj(46, 2)))
                            else:
                                sync.wait_ge(sPE.h, M_PE(pp, PEj(j - 2, 3)))
                                sync.wait_ge(sDVE.h, M_DVE(pp, DVEj(j - 2, 2)))
                            ii = sync.dma_start_transpose(
                                out=smb[:, 0:128], in_=xbs(0)[:, 128 * k:128 * (k + 1)])
                            sQ[0].inc(ii, 16)

                @block.gpsimd
                def _(gp):
                    bwait(gp, sn0)
                    for pp in range(NP):
                        if pp >= 1:
                            gp.wait_ge(sPE.h, M_PE(pp - 1, PE_PP))
                        for u in range(2):
                            ub_d = DVE_SA_U * u
                            ub_a = ACT_SA_U * u
                            gp.wait_ge(sLd.h, M_LD(pp, LD_SA_U * u + 96))
                            if pp >= 1:
                                gp.wait_ge(sQ[u].h, q0[u] + 384 * pp)
                            if pp == 0:
                                ii = gp.memset(h0[:, 1536 * u:1536 * (u + 1)], 0.0)
                                sPool.inc(ii)
                            else:
                                sPool.inc(gp.nop())
                            for hh in range(24):
                                srcp = xpc[:, P3R * u + 128 * (hh // 2):
                                           P3R * u + 128 * (hh // 2) + 128]
                                ii = gp.tensor_copy(
                                    out=xbs(u)[64:128, 128 * hh:128 * (hh + 1)],
                                    in_=srcp[64 * (hh % 2):64 * (hh % 2) + 64, :])
                            sPool.inc(ii)
                            gp.wait_ge(sDVE.h, M_DVE(pp, ub_d + 11))
                            gp.wait_ge(sACT.h, M_ACT(pp, ub_a + 11))
                            gp.tensor_copy(out=qr48[0:24, :],
                                           in_=rr[:, 128:256])
                            ii = gp.tensor_copy(out=qr48[32:56, :],
                                                in_=ecr[:])
                            sPool.inc(ii)
                        gp.wait_ge(sACT.h, M_ACT(pp, 2 * ACT_SA_U))
                        for j in range(JC):
                            u_, h_ = j % 2, j // 2
                            hb = 64 * (h_ % 2)
                            csrc = xpc[hb:hb + 64,
                                       P3R * u_ + 128 * (12 + h_ // 2):
                                       P3R * u_ + 128 * (12 + h_ // 2) + 128]
                            if j >= 2:
                                gp.wait_ge(sPE.h, M_PE(pp, PEj(j - 2, 1)))
                            ii = gp.tensor_tensor(
                                out=cn[hb:hb + 64, 128 * u_:128 * (u_ + 1)],
                                in0=csrc, in1=tailR(u_)[hb:hb + 64, 128 * h_:128 * (h_ + 1)],
                                op=ALU.mult)
                            sPool.inc(ii)
                            if j >= 2:
                                gp.wait_ge(sPE.h, M_PE(pp, PEj(j - 2, 2)))
                            ii = gp.tensor_tensor(
                                out=ctl[:, 128 * u_:128 * (u_ + 1)],
                                in0=csrc, in1=tailE(u_)[hb:hb + 64, 128 * h_:128 * (h_ + 1)],
                                op=ALU.mult)
                            sPool.inc(ii)

                @block.scalar
                def _(sc):
                    bwait(sc, sn0)
                    one24 = ccol[0:24, CC_ONE:CC_ONE + 1]
                    for pp in range(NP):
                        if pp >= 1:
                            sc.wait_ge(sPool.h, M_PO(pp - 1, PO_PP))
                            sc.wait_ge(sDVE.h, M_DVE(pp - 1, 2 * DVE_SA_U))
                        for u in range(2):
                            o = 128 * u
                            o2 = 256 * u
                            ub_d = DVE_SA_U * u
                            ub_p = PE_SA_U * u
                            sc.wait_ge(sDVE.h, M_DVE(pp, ub_d + 2))
                            sACT.inc(sc.activation(out=ntmp[:],
                                                   in_=cstat[:, 0:128],
                                                   func=AF.Ln))
                            sc.wait_ge(sACT.h, sACT.v)
                            sACT.inc(sc.activation(out=nrm[:, 0:128],
                                                   in_=ntmp[:], func=AF.Exp,
                                                   scale=0.5))
                            sACT.inc(sc.activation(out=ntmp[:],
                                                   in_=cstat[:, 128:256],
                                                   func=AF.Ln))
                            sc.wait_ge(sACT.h, sACT.v)
                            sACT.inc(sc.activation(out=nrm[:, 128:256],
                                                   in_=ntmp[:], func=AF.Exp,
                                                   scale=0.5))
                            sc.wait_ge(sDVE.h, M_DVE(pp, ub_d + 4))
                            sACT.inc(sc.activation(out=lnrc[:],
                                                   in_=rr[:, 128:256],
                                                   func=AF.Ln))
                            sc.wait_ge(sDVE.h, M_DVE(pp, ub_d + 5))
                            sACT.inc(sc.activation(out=ltl[:],
                                                   in_=dtb[:], func=AF.Exp))
                            sc.wait_ge(sACT.h, sACT.v)
                            sACT.inc(sc.activation(out=ltl[:],
                                                   in_=ltl[:], func=AF.Ln,
                                                   bias=one24))
                            sc.wait_ge(sDVE.h, M_DVE(pp, ub_d + 6))
                            sACT.inc(sc.activation(out=adec[:],
                                                   in_=att_t[:], func=AF.Exp))
                            sc.wait_ge(sDVE.h, M_DVE(pp, ub_d + 9))
                            sACT.inc(sc.activation(out=tht[:],
                                                   in_=b1t[:], func=AF.Tanh))
                            sc.wait_ge(sDVE.h, M_DVE(pp, ub_d + 10))
                            sACT.inc(sc.activation(out=lat[:],
                                                   in_=alf[:], func=AF.Ln))
                            sc.wait_ge(sDVE.h, M_DVE(pp, ub_d + 13))
                            if u == 1:
                                sc.wait_ge(sPool.h, M_PO(pp, 3))
                            sACT.inc(sc.activation(out=ecr[:],
                                                   in_=cumP[:], func=AF.Exp))
                            for s in range(2):
                                for r in range(6):
                                    sc.wait_ge(sPE.h, M_PE(pp, ub_p + 16 + 6 * s + r + 1))
                                    bank = pb[1 + (6 * s + r) % 3]
                                    tl = tailR(u) if s == 0 else tailE(u)
                                    ii = sc.copy(out=tl[:, 512 * r:512 * (r + 1)],
                                                 in_=bank[:, 0:512])
                                    sACT.inc(ii)
                        if pp >= 1:
                            sc.wait_ge(sDVE.h, M_DVE(pp - 1, DVE_PP))
                        for k in range(KC):
                            j0, j1 = 2 * k, 2 * k + 1
                            if k == 0:
                                sc.wait_ge(sDVE.h, M_DVE(pp, 2 * DVE_SA_U))
                            for j in (j0, j1):
                                u_, h_ = j % 2, j // 2
                                if j >= 2:
                                    sc.wait_ge(sPE.h, M_PE(pp, PEj(j - 2, 0)))
                                ii = sc.activation(out=rtp[:, 128 * u_:128 * (u_ + 1)],
                                                   in_=maskpos[:], func=AF.Identity,
                                                   scale=0.0,
                                                   bias=ctm[:, 64 * u_ + h_:64 * u_ + h_ + 1])
                                sACT.inc(ii)
                            if k == 0:
                                sc.wait_ge(sDVE.h, M_DVE(pp, 32))
                                if pp >= 1:
                                    sc.wait_ge(sPE.h, M_PE(pp - 1, PEj(47, 3)))
                            else:
                                sc.wait_ge(sPE.h, M_PE(pp, PEj(j1 - 2, 3)))
                                sc.wait_ge(sDVE.h, M_DVE(pp, DVEj(j1 - 2, 2)))
                            ii = sc.dma_start_transpose(
                                out=smb[:, 128:256], in_=xbs(1)[:, 128 * k:128 * (k + 1)])
                            sQ[1].inc(ii, 16)
                            sc.wait_ge(sDVE.h, M_DVE(pp, DVEj(j1, 0)))
                            if k >= 1:
                                sc.wait_ge(sDVE.h, M_DVE(pp, DVEj(j1 - 2, 1)))
                            sACT.inc(sc.activation(out=dxp[:], in_=rtp[:], func=AF.Exp))
                            if k >= 1:
                                sc.wait_ge(sDVE.h, M_DVE(pp, DVEj(j0 - 2, 2)))
                            sACT.inc(sc.activation(out=w2c[:, 0:1], in_=rtp[:, 127:128],
                                                   func=AF.Exp))
                            if k >= 1:
                                sc.wait_ge(sDVE.h, M_DVE(pp, DVEj(j0 - 2, 3)))
                            sACT.inc(sc.activation(out=dco[:, 0:1], in_=pRv(0)[:, 127:128],
                                                   func=AF.Exp))
                            if k >= 1:
                                sc.wait_ge(sDVE.h, M_DVE(pp, DVEj(j1 - 2, 2)))
                            sACT.inc(sc.activation(out=w2c[:, 1:2], in_=rtp[:, 255:256],
                                                   func=AF.Exp))
                            if k >= 1:
                                sc.wait_ge(sDVE.h, M_DVE(pp, DVEj(j1 - 2, 3)))
                            sACT.inc(sc.activation(out=dco[:, 1:2], in_=pRv(1)[:, 127:128],
                                                   func=AF.Exp))

                @block.tensor
                def _(te):
                    bwait(te, sn0)
                    for pp in range(NP):
                        te.wait_ge(sDVE.h, dve0 + pp * DVE_PP)
                        te.wait_ge(sACT.h, act0 + pp * ACT_PP)
                        te.wait_ge(sPool.h, po0 + pp * PO_PP)
                        for u in range(2):
                            col0 = colof(pp, u)
                            ub_d = DVE_SA_U * u
                            ub_a = ACT_SA_U * u
                            te.wait_ge(sLd.h, M_LD(pp, LD_SA_U * u + 96))
                            te.wait_ge(sDVE.h, M_DVE(pp, ub_d + 1))
                            if u == 1:
                                te.wait_ge(sDVE.h, M_DVE(pp, 15))
                            for jx in range(KI):
                                ii = te.matmul(pb[0][0:24, 0:128],
                                               indicT_b[:, 24 * jx:24 * (jx + 1)],
                                               sqt[:, 128 * jx:128 * (jx + 1)],
                                               start=(jx == 0), stop=(jx == KI - 1))
                            sPE.inc(ii)
                            for jx in range(KI):
                                ii = te.matmul(pb[0][0:24, 128:256],
                                               indicT_b[:, 24 * jx:24 * (jx + 1)],
                                               sqt[:, 1536 + 128 * jx:1536 + 128 * (jx + 1)],
                                               start=(jx == 0), stop=(jx == KI - 1))
                            sPE.inc(ii)
                            for jx in range(KI):
                                ii = te.matmul(pb[0][0:24, 256:384],
                                               indicT64_b[:, 24 * jx:24 * (jx + 1)],
                                               xln[:, 4096 * jx + col0:4096 * jx + col0 + 128],
                                               start=(jx == 0), stop=(jx == KI - 1))
                            sPE.inc(ii)
                            te.wait_ge(sDVE.h, M_DVE(pp, ub_d + 14))
                            ii = te.transpose(pb[0][:, 384 + 64 * u:448 + 64 * u],
                                              st48[:],
                                              ident_f[0:64, 0:64])
                            sPE.inc(ii)
                            te.wait_ge(sDVE.h, M_DVE(pp, ub_d + 11))
                            if u == 1:
                                te.wait_ge(sACT.h, M_ACT(pp, ACT_SA_U))
                            for jx in range(KI):
                                ii = te.matmul(pb[1 + jx // 4][:, 128 * (jx % 4):
                                                               128 * (jx % 4 + 1)],
                                               indic_b[:, 128 * jx:128 * (jx + 1)],
                                               omal_b[:],
                                               start=True, stop=True)
                                sPE.inc(ii)
                            te.wait_ge(sDVE.h, M_DVE(pp, ub_d + 16))
                            for s in range(2):
                                te.wait_ge(sLd.h, M_LD(pp, LD_SA_U * u + 112 + 16 * (s + 1)))
                                for r in range(6):
                                    g = 12 * u + 6 * s + r
                                    if g >= 3 and 6 * s + r >= 3:
                                        up, tp = divmod(g - 3, 12)
                                        te.wait_ge(sACT.h,
                                                   M_ACT(pp, ACT_SA_U * up + 11 + tp + 1))
                                    bank = pb[1 + (6 * s + r) % 3]
                                    ii = te.matmul(
                                        bank[:, 0:512],
                                        ones_b[0:1, 0:128],
                                        qrowS[0:1, 512 * r:512 * (r + 1)],
                                        start=True, stop=True)
                                    sPE.inc(ii)
                        te.wait_ge(sLd.h, M_LD(pp, 2 * LD_SA_U))
                        te.wait_ge(sPool.h, M_PO(pp, 2 * PO_SA_U))
                        te.wait_ge(sDVE.h, M_DVE(pp, 2 * DVE_SA_U))
                        for k in range(KC):
                            for j in (2 * k, 2 * k + 1):
                                u_, h_ = j % 2, j // 2
                                te.wait_ge(sACT.h, M_ACT(pp, ACTrt(j)))
                                if j >= 2:
                                    te.wait_ge(sDVE.h, M_DVE(pp, DVEj(j - 2, 0)))
                                    te.wait_ge(sDVE.h, M_DVE(pp, DVEj(j - 2, 4)))
                                    te.wait_ge(sACT.h, M_ACT(pp, ACTdco(j - 2)))
                                ii = te.transpose(pRv(u_), rtp[:, 128 * u_:128 * (u_ + 1)],
                                                  ident_f[:])
                                sPE.inc(ii)
                            for j in (2 * k, 2 * k + 1):
                                u_, h_ = j % 2, j // 2
                                hb = 64 * (h_ % 2)
                                te.wait_ge(sPool.h, M_PO(pp, POj(j, 0)))
                                if j >= 2:
                                    te.wait_ge(sDVE.h, M_DVE(pp, DVEj(j - 2, 1)))
                                    te.wait_ge(sDVE.h, M_DVE(pp, DVEj(j - 2, 5)))
                                bsl = xpc[hb:hb + 64,
                                          P3R * u_ + 128 * (h_ // 2):
                                          P3R * u_ + 128 * (h_ // 2) + 128]
                                ii = te.matmul(pGv(u_), bsl,
                                               cn[hb:hb + 64, 128 * u_:128 * (u_ + 1)],
                                               start=True, stop=True)
                                sPE.inc(ii)
                            for j in (2 * k, 2 * k + 1):
                                u_, h_ = j % 2, j // 2
                                hb = 64 * (h_ % 2)
                                te.wait_ge(sDVE.h, M_DVE(pp, DVEj(j, 1)))
                                te.wait_ge(sQ[u_].h, M_Q(u_, pp, h_))
                                te.wait_ge(sPool.h, M_PO(pp, POj(j, 1)))
                                if j >= 2:
                                    te.wait_ge(sDVE.h, M_DVE(pp, DVEj(j - 2, 5)))
                                te.matmul(pYv(u_, hb), smb[:, 128 * u_:128 * u_ + 64],
                                          wtb[:, 128 * u_:128 * (u_ + 1)],
                                          start=True, stop=False)
                                ii = te.matmul(pYv(u_, hb),
                                               h0[:, 1536 * u_ + 64 * h_:
                                                  1536 * u_ + 64 * (h_ + 1)],
                                               ctl[:, 128 * u_:128 * (u_ + 1)],
                                               start=False, stop=True)
                                sPE.inc(ii)
                                te.wait_ge(sDVE.h, M_DVE(pp, DVEj(j, 2)))
                                te.wait_ge(sDVE.h, M_DVE(pp, DVEj(j, 3)))
                                if j >= 2:
                                    te.wait_ge(sDVE.h, M_DVE(pp, DVEj(j - 2, 4)))
                                te.matmul(pHv(u_), btl[:, 64 * u_:64 * (u_ + 1)],
                                          smb[:, 128 * u_:128 * u_ + 64],
                                          start=True, stop=False)
                                ii = te.matmul(pHv(u_), ddt[:, 64 * u_:64 * (u_ + 1)],
                                               h0[:, 1536 * u_ + 64 * h_:
                                                  1536 * u_ + 64 * (h_ + 1)],
                                               start=False, stop=True)
                                sPE.inc(ii)

                @block.vector
                def _(v):
                    bwait(v, sn0)
                    for pp in range(NP):
                        for u in range(2):
                            col0 = colof(pp, u)
                            o = 128 * u
                            o2 = 256 * u
                            o3 = 384 * u
                            ub_p = PE_SA_U * u
                            ub_a = ACT_SA_U * u
                            v.wait_ge(sLd.h, M_LD(pp, LD_SA_U * u + 96))
                            if u == 1:
                                v.wait_ge(sPE.h, M_PE(pp, 2))
                            elif pp >= 1:
                                v.wait_ge(sPE.h, M_PE(pp - 1, PE_SA_U + 2))
                            if pp >= 1:
                                v.wait_ge(sQ[u].h, q0[u] + 384 * pp)
                            ii = v.tensor_tensor(out=sqt[:],
                                                 in0=xpc[:, P3R * u:P3R * u + 3072],
                                                 in1=xpc[:, P3R * u:P3R * u + 3072],
                                                 op=ALU.mult)
                            sDVE.inc(ii)
                            v.wait_ge(sPE.h, M_PE(pp, ub_p + 3))
                            if u == 1:
                                v.wait_ge(sACT.h, M_ACT(pp, 4))
                            v.tensor_copy(out=cstat[:, 0:128], in_=pb[0][0:24, 0:128])
                            v.tensor_copy(out=cstat[:, 128:256],
                                          in_=pb[0][0:24, 128:256])
                            ii = v.tensor_copy(out=cstat[:, 256:384],
                                               in_=pb[0][0:24, 256:384])
                            sDVE.inc(ii)
                            v.wait_ge(sACT.h, M_ACT(pp, ub_a + 4))
                            ii = v.tensor_scalar(out=nrm[:],
                                                 in0=nrm[:],
                                                 scalar1=1e-12, scalar2=None, op0=ALU.max)
                            sDVE.inc(ii)
                            v.wait_ge(sDVE.h, sDVE.v)
                            if u == 1:
                                v.wait_ge(sPool.h, M_PO(pp, 3))
                                v.wait_ge(sACT.h, M_ACT(pp, 5))
                            ii = v.reciprocal(out=rr[:], in_=nrm[:])
                            sDVE.inc(ii)
                            dsl = xpc[0:24, P3R * u + 3072:P3R * u + 3200]
                            v.tensor_scalar(out=dta[:], in0=dsl, scalar1=0.0,
                                            scalar2=None, op0=ALU.max)
                            v.tensor_scalar(out=dtb[:], in0=dsl, scalar1=-1.0,
                                            scalar2=None, op0=ALU.mult)
                            ii = v.tensor_tensor(out=dtb[:], in0=dsl,
                                                 in1=dtb[:], op=ALU.min)
                            sDVE.inc(ii)
                            v.wait_ge(sACT.h, M_ACT(pp, ub_a + 7))
                            v.tensor_tensor(out=dta[:], in0=dta[:],
                                            in1=ltl[:], op=ALU.add)
                            ii = v.tensor_scalar(out=att_t[:],
                                                 in0=dta[:],
                                                 scalar1=v_hd[:, 4:5], scalar2=None,
                                                 op0=ALU.mult)
                            sDVE.inc(ii)
                            v.wait_ge(sACT.h, M_ACT(pp, ub_a + 8))
                            ii = v.tensor_scalar(out=oma_b[:],
                                                 in0=adec[:],
                                                 scalar1=-1.0, scalar2=1.0,
                                                 op0=ALU.mult, op1=ALU.add)
                            sDVE.inc(ii)
                            v.tensor_tensor(out=pt1[:], in0=oma_b[:],
                                            in1=cstat[:, 256:384], op=ALU.mult)
                            v.tensor_tensor(out=pt2[:], in0=pt1[:],
                                            in1=pt1[:], op=ALU.mult)
                            v.tensor_tensor(out=pt1[:], in0=rr[:, 0:128],
                                            in1=rr[:, 0:128], op=ALU.mult)
                            v.scalar_tensor_tensor(out=pt1[:],
                                                   in0=pt1[:],
                                                   scalar=1.0 / 64, in1=cstat[:, 0:128],
                                                   op0=ALU.mult, op1=ALU.mult)
                            ii = v.tensor_tensor(out=pet[:], in0=pt2[:],
                                                 in1=pt1[:], op=ALU.mult)
                            sDVE.inc(ii)
                            ii = v.tensor_scalar(out=b1t[:], in0=pet[:],
                                                 scalar1=v_hd[:, 3:4], scalar2=v_hd[:, 6:7],
                                                 op0=ALU.mult, op1=ALU.mult)
                            sDVE.inc(ii)
                            v.wait_ge(sACT.h, M_ACT(pp, ub_a + 9))
                            v.tensor_scalar(out=alf[:], in0=tht[:],
                                            scalar1=0.0, scalar2=None, op0=ALU.max)
                            v.tensor_scalar(out=alf[:], in0=alf[:],
                                            scalar1=v_hd[:, 7:8], scalar2=v_hd[:, 5:6],
                                            op0=ALU.mult, op1=ALU.add)
                            ii = v.tensor_scalar(out=alf[:], in0=alf[:],
                                                 scalar1=0.01, scalar2=0.999,
                                                 op0=ALU.max, op1=ALU.min)
                            sDVE.inc(ii)
                            ii = v.tensor_scalar(out=omal_b[:],
                                                 in0=alf[:],
                                                 scalar1=-1.0, scalar2=1.0,
                                                 op0=ALU.mult, op1=ALU.add)
                            sDVE.inc(ii)
                            v.wait_ge(sACT.h, M_ACT(pp, ub_a + 10))
                            ii = v.tensor_tensor_scan(out=cum24[:],
                                                      data0=lat[:], data1=zer24[:],
                                                      initial=0.0, op0=ALU.add, op1=ALU.add)
                            sDVE.inc(ii)
                            v.wait_ge(sACT.h, M_ACT(pp, ub_a + 5))
                            ii = v.tensor_tensor(out=cumP[:],
                                                 in0=cum24[:],
                                                 in1=lnrc[:], op=ALU.add)
                            sDVE.inc(ii)
                            if u == 1:
                                v.wait_ge(sPE.h, M_PE(pp, 4))
                            elif pp >= 1:
                                v.wait_ge(sPE.h, M_PE(pp - 1, PE_SA_U + 4))
                            v.tensor_copy(out=st48[0:24, :], in_=cum24[:])
                            ii = v.tensor_copy(out=st48[32:56, :],
                                               in_=rr[:, 0:128])
                            sDVE.inc(ii)
                            v.wait_ge(sPE.h, M_PE(pp, ub_p + 4))
                            ii = v.tensor_copy(out=ctm[:, 64 * u:64 * (u + 1)],
                                               in_=pb[0][:, 384 + 64 * u:448 + 64 * u])
                            sDVE.inc(ii)
                            v.wait_ge(sPE.h, M_PE(pp, ub_p + 16))
                            for hh in range(24):
                                jq = hh // 2
                                pd = pb[1 + jq // 4]
                                ii = v.tensor_tensor(
                                    out=xbs(u)[0:64, 128 * hh:128 * (hh + 1)],
                                    in0=xln[64 * (hh % 2):64 * (hh % 2) + 64,
                                            4096 * jq + col0:4096 * jq + col0 + 128],
                                    in1=pd[64 * (hh % 2):64 * (hh % 2) + 64,
                                           128 * (jq % 4):128 * (jq % 4 + 1)],
                                    op=ALU.mult)
                            sDVE.inc(ii)
                        for k in range(KC):
                            for j in (2 * k, 2 * k + 1):
                                u_, h_ = j % 2, j // 2
                                v.wait_ge(sPE.h, M_PE(pp, PEj(j, 0)))
                                if j >= 2:
                                    v.wait_ge(sACT.h, M_ACT(pp, ACTw2c(j - 2)))
                                ii = v.scalar_tensor_tensor(
                                    out=rtp[:, 128 * u_:128 * (u_ + 1)],
                                    in0=pRv(u_),
                                    scalar=ctm[:, 64 * u_ + h_:64 * u_ + h_ + 1],
                                    in1=maskpos[:],
                                    op0=ALU.subtract, op1=ALU.subtract)
                                sDVE.inc(ii)
                            for j in (2 * k, 2 * k + 1):
                                u_, h_ = j % 2, j // 2
                                hb = 64 * (h_ % 2)
                                v.wait_ge(sACT.h, M_ACT(pp, ACTdexp(k)))
                                v.wait_ge(sPE.h, M_PE(pp, PEj(j, 1)))
                                if j >= 2:
                                    v.wait_ge(sPE.h, M_PE(pp, PEj(j - 2, 2)))
                                ii = v.scalar_tensor_tensor(
                                    out=wtb[:, 128 * u_:128 * (u_ + 1)],
                                    in0=dxp[:, 128 * u_:128 * (u_ + 1)],
                                    scalar=ctm[:, 64 * u_ + 32 + h_:64 * u_ + 33 + h_],
                                    in1=pGv(u_),
                                    op0=ALU.mult, op1=ALU.mult)
                                sDVE.inc(ii)
                                v.wait_ge(sQ[u_].h, M_Q(u_, pp, h_))
                                v.wait_ge(sACT.h, M_ACT(pp, ACTw2c(j)))
                                if j >= 2:
                                    v.wait_ge(sPE.h, M_PE(pp, PEj(j - 2, 3)))
                                ii = v.tensor_scalar(
                                    out=btl[:, 64 * u_:64 * (u_ + 1)],
                                    in0=smb[:, 128 * u_ + 64:128 * (u_ + 1)],
                                    scalar1=ctm[:, 64 * u_ + 32 + h_:64 * u_ + 33 + h_],
                                    scalar2=w2c[:, u_:u_ + 1],
                                    op0=ALU.mult, op1=ALU.mult)
                                sDVE.inc(ii)
                                v.wait_ge(sACT.h, M_ACT(pp, ACTdco(j)))
                                if j >= 2:
                                    v.wait_ge(sPE.h, M_PE(pp, PEj(j - 2, 3)))
                                ii = v.tensor_scalar(out=ddt[:, 64 * u_:64 * (u_ + 1)],
                                                     in0=ident_f[0:64, 0:64],
                                                     scalar1=dco[0:64, u_:u_ + 1],
                                                     scalar2=None, op0=ALU.mult)
                                sDVE.inc(ii)
                                v.wait_ge(sPE.h, M_PE(pp, PEj(j, 3)))
                                ii = v.tensor_copy(
                                    out=h0[:, 1536 * u_ + 64 * h_:1536 * u_ + 64 * (h_ + 1)],
                                    in_=pHv(u_))
                                sDVE.inc(ii)
                                ii = v.tensor_tensor(
                                    out=xln[hb:hb + 64,
                                            4096 * (h_ // 2) + colof(pp, u_):
                                            4096 * (h_ // 2) + colof(pp, u_) + 128],
                                    in0=pYv(u_, hb),
                                    in1=szc[hb:hb + 64,
                                            DI * u_ + 128 * (h_ // 2):
                                            DI * u_ + 128 * (h_ // 2) + 128],
                                    op=ALU.mult)
                                sDVE.inc(ii)

        if phase_on(1):
            emit_ln(hsT, rms=False)
        if phase_on(2):
            emit_gateup(wdprep=True)
        if phase_on(3):
            emit_down(hsT, resA)
        if phase_on(4):
            emit_ln(resA, rms=True)
        if phase_on(5):
            emit_proj(inT, KD, 24, "in")
        if phase_on(6):
            emit_conv()
        if phase_on(7):
            emit_proj(xpTw, KI, KP, "xp")
        if phase_on(8):
            emit_scan(np_cap=scan_np, j_cap=scan_jc)
        if phase_on(9):
            emit_proj(opT, KI, KD, "out", srcres=resA, dstres=resB)
        if phase_on(10):
            emit_ln(resB, rms=False)
        if phase_on(11):
            emit_gateup(wdprep=False)
        if phase_on(12):
            emit_down(resB, out)

        if dev:
            sn1 = snap()

            @block.sync
            def _(sync):
                bwait(sync, sn1)
                for k in range(KD):
                    i = sync.dma_start(out=xln_d[128 * k:128 * (k + 1), :],
                                       in_=xln[:, 4096 * k:4096 * (k + 1)])
                    sSt.inc(i, 16)
                for k in range(KI):
                    i = sync.dma_start(out=xcy_d[128 * k:128 * (k + 1), :],
                                       in_=xln[:, 4096 * k:4096 * (k + 1)])
                    sSt.inc(i, 16)

        sn_end = snap()

        @block.sync
        def _(sync):
            bwait(sync, sn_end)

    return nc


def _prep_host(inputs):
    f32 = np.float32
    att = {k[8:]: v for k, v in inputs.items() if k.startswith("L0_attn_")}
    ffn = {k[7:]: v for k, v in inputs.items() if k.startswith("L0_ffn_")}
    hs = np.asarray(att["hidden_states"], f32)

    com = {}
    com["wgT"] = np.ascontiguousarray(np.asarray(ffn["wq_gate"], np.int32).T)
    com["wuT"] = np.ascontiguousarray(np.asarray(ffn["wq_up"], np.int32).T)
    com["wdT"] = np.ascontiguousarray(np.asarray(ffn["wq_down"], np.int32).T)
    com["sgate"] = np.ascontiguousarray(np.asarray(ffn["s_gate"], f32).reshape(KH, 128).T)
    com["sup"] = np.ascontiguousarray(np.asarray(ffn["s_up"], f32).reshape(KH, 128).T)
    com["sdown"] = np.ascontiguousarray(np.asarray(ffn["s_down"], f32).reshape(KD, 128).T)
    com["inT"] = np.ascontiguousarray(np.asarray(att["in_proj_w"], f32).T)
    com["inb"] = np.ascontiguousarray(np.asarray(att["in_proj_b"], f32).reshape(24, 128).T)

    xw = np.asarray(att["x_proj_w"], f32)
    xb = np.asarray(att["x_proj_b"], f32)
    perm = np.zeros(P3R, np.int64)
    wnew = np.zeros((P3R, DI), f32)
    bnew = np.zeros(P3R, f32)
    for h in range(H):
        base = h * 129
        wnew[64 * h:64 * (h + 1)] = xw[base + 1:base + 65]
        bnew[64 * h:64 * (h + 1)] = xb[base + 1:base + 65]
        wnew[1536 + 64 * h:1536 + 64 * (h + 1)] = xw[base + 65:base + 129]
        bnew[1536 + 64 * h:1536 + 64 * (h + 1)] = xb[base + 65:base + 129]
        wnew[3072 + h] = xw[base]
        bnew[3072 + h] = xb[base]
    com["xpTw"] = np.ascontiguousarray(wnew.T)
    com["xpb"] = np.ascontiguousarray(bnew.reshape(KP, 128).T)

    com["opT"] = np.ascontiguousarray(np.asarray(att["out_proj_w"], f32).T)
    com["opb"] = np.ascontiguousarray(np.asarray(att["out_proj_b"], f32).reshape(KD, 128).T)
    cw = np.asarray(att["conv_w"], f32).reshape(DI, 4)
    com["convw"] = np.ascontiguousarray(cw.reshape(KI, 128, 4).transpose(1, 0, 2).reshape(128, KI * 4))
    com["convb"] = np.ascontiguousarray(np.asarray(att["conv_b"], f32).reshape(KI, 128).T)
    com["normw"] = np.ascontiguousarray(np.asarray(att["norm_w"], f32).reshape(KD, 128).T)
    com["headv"] = np.ascontiguousarray(np.stack(
        [np.asarray(att["A_log"], f32), np.asarray(att["log2_alpha_base"], f32),
         np.asarray(att["log2_beta"], f32), np.asarray(att["surprise_ema"], f32)], axis=1))
    com["resg"] = np.asarray(att["res_gate"], f32).reshape(1, 1)

    com["ones_h"] = np.ones((128, 128), f32)
    ind = np.zeros((24, DI), f32)
    for h in range(H):
        ind[h, 64 * h:64 * (h + 1)] = 1.0
    com["indic_h"] = ind
    it = np.zeros((128, KI * 24), f32)
    it64 = np.zeros((128, KI * 24), f32)
    for j in range(KI):
        for p in range(128):
            h = 2 * j + p // 64
            it[p, 24 * j + h] = 1.0
            it64[p, 24 * j + h] = 1.0 / 64
    com["indicT_h"] = it
    com["indicT64_h"] = it64
    tau = np.arange(128)[:, None]
    tt = np.arange(128)[None, :]
    com["maskpos_h"] = np.where(tau > tt, 1e30, 0.0).astype(f32)
    com["ident_h"] = np.eye(128, dtype=f32)
    cc = np.zeros((128, NCC), f32)
    cc[:, CC_M128] = -128.0
    cc[:, CC_ONE] = 1.0
    cc[:, CC_LNEPS] = LN_EPS
    cc[:, CC_RMSB] = LN_EPS * RMS_EPS
    com["ccol_h"] = cc
    return hs, com


def kernel(**inputs):
    hs, com = _prep_host(inputs)
    nc = build_program()
    in_maps = []
    for c in range(8):
        m = dict(com)
        m["hsT"] = np.ascontiguousarray(
            np.concatenate([hs[2 * c].T, hs[2 * c + 1].T], axis=1))
        in_maps.append(m)
    res = run_bass_kernel_spmd(nc, in_maps, core_ids=list(range(8)))
    outs = np.empty((16, L, D), np.float32)
    for c in range(8):
        o = res.results[c]["out"]
        outs[2 * c] = o[:, 0:L].T
        outs[2 * c + 1] = o[:, L:2 * L].T
    return outs


# version 10 (final)
# speedup vs baseline: 1.0523x; 1.0523x over previous
import numpy as np
import ml_dtypes
from contextlib import ExitStack

import concourse.bass as bass
import concourse.mybir as mybir
from concourse.bass_utils import run_bass_kernel_spmd

F32 = mybir.dt.float32
BF16 = mybir.dt.bfloat16
I32 = mybir.dt.int32
AF = mybir.ActivationFunctionType
ALU = mybir.AluOpType

D = 2048
L = 2048
NB = 2
T = NB * L
HID = 5632
DI = 1536
H = 24
S = 64
DH = 64
P3R = 3200
KD = D // 128
KH = HID // 128
KI = DI // 128
KP = P3R // 128
CHK = 128
NCH = T // CHK
PADL = 3
XPW = PADL + L
LN_EPS = 1e-5
RMS_EPS = float(np.finfo(np.float32).eps)
LN2 = float(np.log(2.0))

CC_M128, CC_ONE, CC_LNEPS, CC_RMSB = 0, 1, 2, 3
NCC = 4

PE_A, PE_H = 9, 5
ACT_A = 22
DVE_A, DVE_H = 15, 7
PE_PC = PE_A + 24 * PE_H
ACT_PC = 118
DVE_PC = DVE_A + 24 * DVE_H
PO_PC = 51
LD_PC = 144


def ACT_rtc(h):
    return 23 if h == 0 else 20 + 4 * h


def ACT_dexp(h):
    return 116 if h == 23 else 25 + 4 * h


def ACT_w2e(h):
    return 117 if h == 23 else 26 + 4 * h


def ACT_ece(h):
    return 118 if h == 23 else 27 + 4 * h


def PO_ctl(h):
    return 4 if h == 0 else 2 * h + 3


def PO_yw(h):
    return 51 if h == 23 else 2 * h + 6


class Sem:
    def __init__(self, ctx, nc, name):
        self.h = ctx.enter_context(nc.semaphore(name))
        self.v = 0

    def inc(self, instr, n=1):
        instr.then_inc(self.h, n)
        self.v += n


def build_program(phases_upto=99, dev=False, zset=(), scan_np=None, scan_jc=None):
    nc = bass.Bass("TRN2", target_bir_lowering=False)

    _wt = {}

    def wge(eng, sem, val):
        key = (id(eng), id(sem))
        if _wt.get(key, -1) >= val:
            return
        _wt[key] = val
        eng.wait_ge(sem, val)

    def din(name, shape, dt):
        return nc.dram_tensor(name, shape, dt, kind="ExternalInput").ap()

    hsT = din("hsT", [D, T], F32)
    wgT = din("wgT", [D, HID], I32)
    wuT = din("wuT", [D, HID], I32)
    wdT = din("wdT", [HID, D], I32)
    sgate = din("sgate", [128, KH], F32)
    sup = din("sup", [128, KH], F32)
    sdown = din("sdown", [128, KD], F32)
    inT = din("inT", [D, 2 * DI], F32)
    inb = din("inb", [128, 24], F32)
    xpTw = din("xpTw", [DI, P3R], F32)
    xpb = din("xpb", [128, KP], F32)
    opT = din("opT", [DI, D], F32)
    opb = din("opb", [128, KD], F32)
    convw = din("convw", [128, KI * 4], F32)
    convb = din("convb", [128, KI], F32)
    normw = din("normw", [128, KD], F32)
    indic_h = din("indic_h", [24, DI], F32)
    headv = din("headv", [24, 4], F32)
    resg = din("resg", [1, 1], F32)
    ones_h = din("ones_h", [128, 128], F32)
    indicT_h = din("indicT_h", [128, KI * 24], F32)
    indicT64_h = din("indicT64_h", [128, KI * 24], F32)
    maskpos_h = din("maskpos_h", [128, 128], F32)
    ident_h = din("ident_h", [128, 128], F32)
    ccol_h = din("ccol_h", [128, NCC], F32)

    out = nc.dram_tensor("out", [D, T], F32, kind="ExternalOutput").ap()
    def dk(name):
        return "ExternalOutput" if (dev or name in zset) else "Internal"
    resA = nc.dram_tensor("resA", [D, T], F32, kind=dk("resA")).ap()
    resB = nc.dram_tensor("resB", [D, T], F32, kind=dk("resB")).ap()
    g_bf = nc.dram_tensor("g_bf", [HID, T], BF16, kind=dk("g_bf")).ap()
    wd_bf2 = nc.dram_tensor("wd_bf2", [KD, HID, 128], BF16, kind=dk("wd_bf2")).ap()
    xpad = nc.dram_tensor("xpad", [DI, NB * XPW], BF16, kind=dk("xpad")).ap()
    siluzT = nc.dram_tensor("siluzT", [DI, T], BF16, kind=dk("siluzT")).ap()
    xpT_s = nc.dram_tensor("xpT_s", [P3R, T], BF16, kind=dk("xpT_s")).ap()
    qrow_d = nc.dram_tensor("qrow_d", [64, 128], BF16, kind="Internal").ap()
    lnm_d = nc.dram_tensor("lnm_d", [128, T], F32, kind="Internal").ap()
    lnr_d = nc.dram_tensor("lnr_d", [128, T], F32, kind="Internal").ap()
    if dev:
        xln_d = nc.dram_tensor("xln_d", [D, T], BF16, kind="ExternalOutput").ap()
        xcy_d = nc.dram_tensor("xcy_d", [DI, T], BF16, kind="ExternalOutput").ap()

    ctx = ExitStack()
    with ctx:
        sLd = Sem(ctx, nc, "sLd")
        sSc = Sem(ctx, nc, "sSc")
        sWl = Sem(ctx, nc, "sWl")
        sSt = Sem(ctx, nc, "sSt")
        sPE = Sem(ctx, nc, "sPE")
        sACT = Sem(ctx, nc, "sACT")
        sDVE = Sem(ctx, nc, "sDVE")
        sPool = Sem(ctx, nc, "sPool")
        sQ = [Sem(ctx, nc, "sQ0"), Sem(ctx, nc, "sQ1")]
        sR = [Sem(ctx, nc, "sR0"), Sem(ctx, nc, "sR1")]
        ALLS = [sLd, sSc, sWl, sSt, sPE, sACT, sDVE, sPool] + sQ + sR

        def snap():
            return [(s, s.v) for s in ALLS]

        def bwait(eng, sn):
            for s, v in sn:
                if v > 0:
                    wge(eng, s.h, v)

        sb = lambda name, shape, dt=F32: ctx.enter_context(nc.sbuf_tensor(name, shape, dt))
        xln = sb("xln", [128, KD * 4096], BF16)
        ones_b = sb("ones_b", [128, 128], BF16)
        indic_b = sb("indic_b", [24, DI], BF16)
        ones_f = sb("ones_f", [128, 128], F32)
        indicT_b = sb("indicT_b", [128, KI * 24], BF16)
        indicT64_b = sb("indicT64_b", [128, KI * 24], BF16)
        maskpos = sb("maskpos", [128, 128], F32)
        ident_f = sb("ident_f", [128, 128], F32)
        ccol = sb("ccol", [128, NCC], F32)
        v_sg = sb("v_sg", [128, KH], F32)
        v_suh = sb("v_suh", [128, KH], F32)
        v_sd = sb("v_sd", [128, KD], F32)
        v_inb = sb("v_inb", [128, 24], F32)
        v_xpb = sb("v_xpb", [128, KP], F32)
        v_opb = sb("v_opb", [128, KD], F32)
        v_cw = sb("v_cw", [128, KI * 4], F32)
        v_cb = sb("v_cb", [128, KI], F32)
        v_nw = sb("v_nw", [128, KD], F32)
        v_hd = sb("v_hd", [24, 8], F32)
        rg_col = sb("rg_col", [128, 1], F32)
        zer24 = sb("zer24", [24, CHK], F32)
        zcol = sb("zcol", [128, 8], BF16)

        pb = [ctx.enter_context(nc.psum_tensor(f"pb{i}", [128, 512], F32)) for i in range(8)]

        block = ctx.enter_context(nc.Block())
        PH = [0]

        def uname(n_):
            PH[0] += 1
            return f"{n_}_{PH[0]}"

        def phase_on(p):
            return p <= phases_upto

        @block.sync
        def _(sync):
            for t_, a_ in ((ones_f, ones_h), (maskpos, maskpos_h), (ident_f, ident_h),
                           (ccol, ccol_h), (v_sg, sgate), (v_suh, sup), (v_sd, sdown),
                           (v_inb, inb), (v_xpb, xpb), (v_opb, opb), (v_cw, convw),
                           (v_cb, convb), (v_nw, normw)):
                sLd.inc(sync.dma_start(out=t_[:], in_=a_), 16)
            sLd.inc(sync.dma_start(out=v_hd[:, 0:4], in_=headv), 16)
            sLd.inc(sync.dma_start(out=rg_col[0:1, 0:1], in_=resg), 16)

        @block.gpsimd
        def _(gp):
            for t_, a_ in ((ones_b, ones_h), (indic_b, indic_h), (indicT_b, indicT_h),
                           (indicT64_b, indicT64_h)):
                sWl.inc(gp.dma_start(out=t_[:], in_=a_), 16)
            gp.memset(zer24[:], 0.0)
            sPool.inc(gp.memset(zcol[:], 0.0))

        @block.vector
        def _(v):
            wge(v, sLd.h, sLd.v)
            wge(v, sWl.h, sWl.v)
            v.tensor_scalar(out=v_suh[:], in0=v_suh[:], scalar1=0.5, scalar2=None, op0=ALU.mult)
            v.tensor_scalar(out=v_hd[:, 4:5], in0=v_hd[:, 0:1], scalar1=-2.3, scalar2=-0.01,
                            op0=ALU.max, op1=ALU.min)
            v.tensor_scalar(out=v_hd[:, 5:6], in0=v_hd[:, 1:2], scalar1=-3.32, scalar2=-0.015,
                            op0=ALU.max, op1=ALU.min)
            i = v.tensor_scalar(out=v_hd[:, 6:7], in0=v_hd[:, 2:3], scalar1=-2.0, scalar2=2.0,
                                op0=ALU.max, op1=ALU.min)
            sDVE.inc(i)

        @block.scalar
        def _(sc):
            wge(sc, sDVE.h, sDVE.v)
            sc.activation(out=v_hd[:, 5:6], in_=v_hd[:, 5:6], func=AF.Exp, scale=LN2)
            i = sc.activation(out=v_hd[:, 6:7], in_=v_hd[:, 6:7], func=AF.Exp, scale=LN2)
            sACT.inc(i)

        @block.tensor
        def _(te):
            wge(te, sLd.h, sLd.v)
            i = te.matmul(pb[0][:, 0:1], ones_f[0:1, :], rg_col[0:1, 0:1], start=True, stop=True)
            sPE.inc(i)

        @block.vector
        def _(v):
            wge(v, sACT.h, sACT.v)
            wge(v, sPE.h, sPE.v)
            v.tensor_scalar(out=v_hd[:, 5:6], in0=v_hd[:, 5:6], scalar1=-1.0, scalar2=1.0,
                            op0=ALU.mult, op1=ALU.add)
            v.tensor_scalar(out=v_hd[:, 7:8], in0=v_hd[:, 5:6], scalar1=-1.0, scalar2=1.0,
                            op0=ALU.mult, op1=ALU.add)
            i = v.tensor_scalar(out=v_hd[:, 3:4], in0=v_hd[:, 3:4], scalar1=1e-6, scalar2=None,
                                op0=ALU.add)
            sDVE.inc(i)
            wge(v, sDVE.h, sDVE.v)
            v.reciprocal(out=v_hd[:, 3:4], in_=v_hd[:, 3:4])
            i = v.tensor_copy(out=rg_col[:], in_=pb[0][:, 0:1])
            sDVE.inc(i)
            wge(v, sDVE.h, sDVE.v)
            i = v.tensor_scalar(out=v_opb[:], in0=v_opb[:], scalar1=rg_col[:, 0:1], scalar2=None,
                                op0=ALU.mult)
            sDVE.inc(i)

        @block.sync
        def _(sync):
            wge(sync, sPool.h, sPool.v)
            for c in range(KI):
                for b in range(NB):
                    i = sync.dma_start(out=xpad[128 * c:128 * (c + 1), b * XPW:b * XPW + PADL],
                                       in_=zcol[:, 0:PADL])
                    sSt.inc(i, 16)

        def emit_ln(src, rms):
            NLN = T // 512
            APN = 4 if rms else 3
            with ExitStack() as px:
                psb = lambda n_, s_, d_=F32: px.enter_context(nc.sbuf_tensor(uname(n_), s_, d_))
                resn = psb("ln_resn", [128, KD * 512], F32)
                rbt = psb("ln_rbt", [128, KD * 512], BF16)
                rsq = psb("ln_rsq", [128, KD * 512], BF16)
                mrep = psb("ln_mrep", [128, 512], F32)
                srep = psb("ln_srep", [128, 512], F32)
                lt1 = psb("ln_t1", [128, 512], F32)
                lt2 = psb("ln_t2", [128, 512], F32)

                sn0 = snap()
                ld0, pe0, act0, dve0, po0, st0 = sLd.v, sPE.v, sACT.v, sDVE.v, sPool.v, sSt.v
                scale_a = (1.0 + RMS_EPS) if rms else 1.0
                bias_cc = CC_RMSB if rms else CC_LNEPS

                @block.sync
                def _(sync):
                    bwait(sync, sn0)
                    for n in range(NLN):
                        if n >= 1:
                            wge(sync, sDVE.h, dve0 + 3 * n)
                        i = sync.dma_start(
                            out=resn[:, 0:6 * 512].rearrange("p (k c) -> p k c", c=512),
                            in_=src[:, 512 * n:512 * (n + 1)].rearrange(
                                "(k p) c -> p k c", p=128)[:, 0:6])
                        sLd.inc(i, 16)
                        if rms:
                            wge(sync, sDVE.h, dve0 + 3 * n + 2)
                            i = sync.dma_start(out=lnm_d[:, 512 * n:512 * (n + 1)], in_=mrep[:])
                            sSt.inc(i, 16)
                            wge(sync, sSt.h, sSt.v)
                            i = sync.dma_start(out=lnr_d[:, 512 * n:512 * (n + 1)], in_=lt1[:])
                            sSt.inc(i, 16)
                            wge(sync, sSt.h, sSt.v)

                @block.scalar
                def _(sc):
                    bwait(sc, sn0)
                    wl0, sc0 = sWl.v, sSc.v
                    for n in range(NLN):
                        if n >= 1:
                            wge(sc, sDVE.h, dve0 + 3 * n)
                        i = sc.dma_start(
                            out=resn[:, 11 * 512:16 * 512].rearrange("p (k c) -> p k c", c=512),
                            in_=src[:, 512 * n:512 * (n + 1)].rearrange(
                                "(k p) c -> p k c", p=128)[:, 11:16])
                        sSc.inc(i, 16)
                        wge(sc, sLd.h, ld0 + 16 * (n + 1))
                        wge(sc, sWl.h, wl0 + 16 * (n + 1))
                        wge(sc, sSc.h, sc0 + 16 * (n + 1))
                        sACT.inc(sc.copy(out=rbt[:], in_=resn[:]))
                        sACT.inc(sc.activation(out=rsq[:], in_=resn[:], func=AF.Square))
                        wge(sc, sDVE.h, dve0 + 3 * n + 1)
                        i = sc.activation(out=lt2[:], in_=lt1[:], func=AF.Sqrt,
                                          scale=scale_a, bias=ccol[:, bias_cc:bias_cc + 1])
                        sACT.inc(i)
                        if rms:
                            i = sc.activation(out=srep[:], in_=lt1[:], func=AF.Sqrt,
                                              bias=ccol[:, CC_LNEPS:CC_LNEPS + 1])
                            sACT.inc(i)

                @block.tensor
                def _(te):
                    bwait(te, sn0)
                    for n in range(NLN):
                        pS1 = pb[4] if n % 2 == 0 else pb[6]
                        pS2 = pb[5] if n % 2 == 0 else pb[7]
                        wge(te, sACT.h, act0 + APN * n + 2)
                        if n >= 2:
                            wge(te, sDVE.h, dve0 + 3 * (n - 1) + 1)
                        for k in range(KD):
                            te.matmul(pS1[:], ones_b[:], rbt[:, 512 * k:512 * (k + 1)],
                                      start=(k == 0), stop=(k == KD - 1))
                            i = te.matmul(pS2[:], ones_b[:], rsq[:, 512 * k:512 * (k + 1)],
                                          start=(k == 0), stop=(k == KD - 1))
                        sPE.inc(i)

                @block.vector
                def _(v):
                    bwait(v, sn0)
                    for n in range(NLN):
                        pS1 = pb[4] if n % 2 == 0 else pb[6]
                        pS2 = pb[5] if n % 2 == 0 else pb[7]
                        wge(v, sPE.h, pe0 + n + 1)
                        if rms and n >= 1:
                            wge(v, sSt.h, st0 + 32 * n)
                        v.tensor_scalar(out=mrep[:], in0=pS1[:], scalar1=1.0 / D, scalar2=None,
                                        op0=ALU.mult)
                        v.tensor_scalar(out=lt1[:], in0=pS2[:], scalar1=1.0 / D, scalar2=None,
                                        op0=ALU.mult)
                        v.tensor_tensor(out=lt2[:], in0=mrep[:], in1=mrep[:], op=ALU.mult)
                        i = v.tensor_tensor(out=lt1[:], in0=lt1[:], in1=lt2[:], op=ALU.subtract)
                        sDVE.inc(i)
                        wge(v, sACT.h, act0 + APN * (n + 1))
                        if rms:
                            v.reciprocal(out=lt1[:], in_=srep[:])
                            i = v.reciprocal(out=srep[:], in_=lt2[:])
                        else:
                            i = v.reciprocal(out=srep[:], in_=lt2[:])
                        sDVE.inc(i)
                        wge(v, sDVE.h, sDVE.v)
                        wge(v, sPool.h, po0 + n + 1)
                        for k in range(KD):
                            dst = xln[:, 4096 * k + 512 * n:4096 * k + 512 * (n + 1)]
                            if rms:
                                i = v.scalar_tensor_tensor(
                                    out=dst, in0=dst, scalar=v_nw[:, k:k + 1], in1=srep[:],
                                    op0=ALU.mult, op1=ALU.mult)
                            else:
                                i = v.tensor_tensor(out=dst, in0=dst, in1=srep[:], op=ALU.mult)
                        sDVE.inc(i)

                @block.gpsimd
                def _(gp):
                    bwait(gp, sn0)
                    for n in range(NLN):
                        if n >= 1:
                            wge(gp, sDVE.h, dve0 + 3 * n)
                        i = gp.dma_start(
                            out=resn[:, 6 * 512:11 * 512].rearrange("p (k c) -> p k c", c=512),
                            in_=src[:, 512 * n:512 * (n + 1)].rearrange(
                                "(k p) c -> p k c", p=128)[:, 6:11])
                        sWl.inc(i, 16)
                        wge(gp, sDVE.h, dve0 + 3 * n + 1)
                        for k in range(KD):
                            i = gp.tensor_tensor(
                                out=xln[:, 4096 * k + 512 * n:4096 * k + 512 * (n + 1)],
                                in0=resn[:, 512 * k:512 * (k + 1)],
                                in1=mrep[:], op=ALU.subtract)
                        sPool.inc(i)

        def emit_gateup(wdprep):
            with ExitStack() as px:
                psb = lambda n_, s_, d_=F32: px.enter_context(nc.sbuf_tensor(uname(n_), s_, d_))
                wgi = psb("gu_wgi", [128, 2 * 2048], BF16)
                wui = psb("gu_wui", [128, 2 * 2048], BF16)
                wgb = psb("gu_wgb", [128, 2 * 2048], BF16)
                wub = psb("gu_wub", [128, 2 * 2048], BF16)
                sgt = psb("gu_sgt", [128, 2 * 1024], BF16)
                gt = psb("gu_gt", [128, 2 * 1024], BF16)
                wdi = psb("gu_wdi", [128, 2 * 2048], BF16) if wdprep else None
                wdb = psb("gu_wdb", [128, 2 * 2048], BF16) if wdprep else None

                sn0 = snap()
                wl0, pe0, act0, dve0, st0, sc0 = sWl.v, sPE.v, sACT.v, sDVE.v, sSt.v, sSc.v
                APM = 7 if wdprep else 6
                WPM = 48 if wdprep else 32

                @block.gpsimd
                def _(gp):
                    bwait(gp, sn0)
                    for m in range(KH):
                        s = (m % 2)
                        if m >= 2:
                            wge(gp, sACT.h, act0 + APM * (m - 1) + 2 + (1 if wdprep else 0))
                        i = gp.dma_start(
                            out=wgi[:, 2048 * s:2048 * (s + 1)].rearrange("p (k c) -> p k c", c=128),
                            in_=wgT[:, 128 * m:128 * (m + 1)].rearrange("(k p) c -> p k c", p=128))
                        sWl.inc(i, 16)
                        i = gp.dma_start(
                            out=wui[:, 2048 * s:2048 * (s + 1)].rearrange("p (k c) -> p k c", c=128),
                            in_=wuT[:, 128 * m:128 * (m + 1)].rearrange("(k p) c -> p k c", p=128))
                        sWl.inc(i, 16)
                        if wdprep:
                            i = gp.dma_start(out=wdi[:, 2048 * s:2048 * (s + 1)],
                                             in_=wdT[128 * m:128 * (m + 1), :])
                            sWl.inc(i, 16)

                @block.scalar
                def _(sc):
                    bwait(sc, sn0)
                    for m in range(KH):
                        s = (m % 2)
                        wge(sc, sWl.h, wl0 + WPM * (m + 1))
                        sACT.inc(sc.activation(out=wgb[:, 2048 * s:2048 * (s + 1)],
                                               in_=wgi[:, 2048 * s:2048 * (s + 1)],
                                               func=AF.Copy, bias=-128.0))
                        sACT.inc(sc.activation(out=wub[:, 2048 * s:2048 * (s + 1)],
                                               in_=wui[:, 2048 * s:2048 * (s + 1)],
                                               func=AF.Copy, bias=-128.0))
                        if wdprep:
                            if m >= 2:
                                wge(sc, sSc.h, sc0 + 256 * m)
                            sACT.inc(sc.activation(out=wdb[:, 2048 * s:2048 * (s + 1)],
                                                   in_=wdi[:, 2048 * s:2048 * (s + 1)],
                                                   func=AF.Copy, bias=-128.0))
                            wge(sc, sACT.h, sACT.v)
                            for md in range(KD):
                                i = sc.dma_start(
                                    out=wd_bf2[md, 128 * m:128 * (m + 1), :],
                                    in_=wdb[:, 2048 * s + 128 * md:2048 * s + 128 * (md + 1)])
                                sSc.inc(i, 16)
                        for q in range(4):
                            qi = 4 * m + q
                            wge(sc, sPE.h, pe0 + qi + 1)
                            if qi >= 2:
                                wge(sc, sDVE.h, dve0 + qi - 1)
                            p = qi % 2
                            pg0, pg1 = pb[0 + 4 * p], pb[1 + 4 * p]
                            sc.activation(out=sgt[:, 1024 * (qi % 2):1024 * (qi % 2) + 512],
                                          in_=pg0[:], func=AF.Silu, scale=v_sg[:, m:m + 1])
                            i = sc.activation(out=sgt[:, 1024 * (qi % 2) + 512:1024 * (qi % 2 + 1)],
                                              in_=pg1[:], func=AF.Silu, scale=v_sg[:, m:m + 1])
                            sACT.inc(i)

                @block.tensor
                def _(te):
                    bwait(te, sn0)
                    for m in range(KH):
                        s = (m % 2)
                        wge(te, sACT.h, act0 + APM * m + 2)
                        for q in range(4):
                            qi = 4 * m + q
                            if qi >= 2:
                                wge(te, sDVE.h, dve0 + qi - 1)
                            p = qi % 2
                            pg0, pg1 = pb[0 + 4 * p], pb[1 + 4 * p]
                            pu0, pu1 = pb[2 + 4 * p], pb[3 + 4 * p]
                            for k in range(KD):
                                wg_k = wgb[:, 2048 * s + 128 * k:2048 * s + 128 * (k + 1)]
                                wu_k = wub[:, 2048 * s + 128 * k:2048 * s + 128 * (k + 1)]
                                x0 = xln[:, 4096 * k + 1024 * q:4096 * k + 1024 * q + 512]
                                x1 = xln[:, 4096 * k + 1024 * q + 512:4096 * k + 1024 * (q + 1)]
                                te.matmul(pg0[:], wg_k, x0, start=(k == 0), stop=(k == KD - 1))
                                te.matmul(pg1[:], wg_k, x1, start=(k == 0), stop=(k == KD - 1))
                                te.matmul(pu0[:], wu_k, x0, start=(k == 0), stop=(k == KD - 1))
                                i = te.matmul(pu1[:], wu_k, x1, start=(k == 0), stop=(k == KD - 1))
                            sPE.inc(i)

                @block.vector
                def _(v):
                    bwait(v, sn0)
                    for m in range(KH):
                        for q in range(4):
                            qi = 4 * m + q
                            wge(v, sACT.h, act0 + APM * m + 2 + (1 if wdprep else 0) + q + 1)
                            wge(v, sPE.h, pe0 + qi + 1)
                            if qi >= 2:
                                wge(v, sSt.h, st0 + 16 * (qi - 1))
                            p = qi % 2
                            pu0, pu1 = pb[2 + 4 * p], pb[3 + 4 * p]
                            gs = 1024 * (qi % 2)
                            v.scalar_tensor_tensor(out=gt[:, gs:gs + 512], in0=pu0[:],
                                                   scalar=v_suh[:, m:m + 1],
                                                   in1=sgt[:, gs:gs + 512],
                                                   op0=ALU.mult, op1=ALU.mult)
                            i = v.scalar_tensor_tensor(out=gt[:, gs + 512:gs + 1024], in0=pu1[:],
                                                       scalar=v_suh[:, m:m + 1],
                                                       in1=sgt[:, gs + 512:gs + 1024],
                                                       op0=ALU.mult, op1=ALU.mult)
                            sDVE.inc(i)

                @block.sync
                def _(sync):
                    bwait(sync, sn0)
                    for m in range(KH):
                        for q in range(4):
                            qi = 4 * m + q
                            wge(sync, sDVE.h, dve0 + qi + 1)
                            gs = 1024 * (qi % 2)
                            i = sync.dma_start(out=g_bf[128 * m:128 * (m + 1),
                                                        1024 * q:1024 * (q + 1)],
                                               in_=gt[:, gs:gs + 1024])
                            sSt.inc(i, 16)
                            wge(sync, sSt.h, sSt.v)

        def emit_down(srcres, dstres):
            with ExitStack() as px:
                psb = lambda n_, s_, d_=F32: px.enter_context(nc.sbuf_tensor(uname(n_), s_, d_))
                wdb = psb("dn_wdb", [128, 2 * KH * 128], BF16)
                resd = psb("dn_resd", [128, 2 * 1024], F32)
                nres = psb("dn_nres", [128, 2 * 1024], F32)

                sn0 = snap()
                ld0, pe0, dve0, st0 = sLd.v, sPE.v, sDVE.v, sSt.v
                q0 = [sQ[0].v, sQ[1].v]
                r0 = [sR[0].v, sR[1].v]

                @block.sync
                def _(sync):
                    bwait(sync, sn0)
                    for q in range(4):
                        if q >= 1:
                            wge(sync, sPE.h, pe0 + 16 * q)
                        for k0, k1 in ((0, 11), (11, 22), (22, 33), (33, 44)):
                            i = sync.dma_start(
                                out=xln[:, 1024 * k0:1024 * k1].rearrange(
                                    "p (k c) -> p k c", c=1024),
                                in_=g_bf[128 * k0:128 * k1,
                                         1024 * q:1024 * (q + 1)].rearrange(
                                    "(k p) c -> p k c", p=128))
                            sLd.inc(i, 16)
                        for md in range(KD):
                            t = 16 * q + md
                            wge(sync, sDVE.h, dve0 + t + 1)
                            i = sync.dma_start(out=dstres[128 * md:128 * (md + 1),
                                                          1024 * q:1024 * (q + 1)],
                                               in_=nres[:, 1024 * (t % 2):1024 * (t % 2 + 1)])
                            sSt.inc(i, 16)
                            wge(sync, sSt.h, sSt.v)

                @block.scalar
                def _(sc):
                    bwait(sc, sn0)
                    for q in range(4):
                        for md in range(KD):
                            t = 16 * q + md
                            s = md % 2
                            if t >= 2:
                                wge(sc, sPE.h, pe0 + t - 1)
                            i = sc.dma_start(
                                out=wdb[:, KH * 128 * s:KH * 128 * (s + 1)].rearrange(
                                    "p (k c) -> p k c", c=128),
                                in_=wd_bf2[md].rearrange("(k p) c -> p k c", p=128))
                            sQ[s].inc(i, 16)
                            if t >= 2:
                                wge(sc, sDVE.h, dve0 + t - 1)
                            i = sc.dma_start(out=resd[:, 1024 * s:1024 * (s + 1)],
                                             in_=srcres[128 * md:128 * (md + 1),
                                                        1024 * q:1024 * (q + 1)])
                            sR[s].inc(i, 16)

                @block.tensor
                def _(te):
                    bwait(te, sn0)
                    for q in range(4):
                        for md in range(KD):
                            t = 16 * q + md
                            s = md % 2
                            wge(te, sQ[s].h, q0[s] + 16 * (t // 2 + 1))
                            if md == 0:
                                wge(te, sLd.h, ld0 + 64 * (q + 1))
                            if t >= 2:
                                wge(te, sDVE.h, dve0 + t - 1)
                            pA = pb[0 + 2 * s]
                            pB = pb[1 + 2 * s]
                            for k in range(KH):
                                wk = wdb[:, KH * 128 * s + 128 * k:KH * 128 * s + 128 * (k + 1)]
                                te.matmul(pA[:], wk, xln[:, 1024 * k:1024 * k + 512],
                                          start=(k == 0), stop=(k == KH - 1))
                                i = te.matmul(pB[:], wk, xln[:, 1024 * k + 512:1024 * (k + 1)],
                                              start=(k == 0), stop=(k == KH - 1))
                            sPE.inc(i)

                @block.vector
                def _(v):
                    bwait(v, sn0)
                    for q in range(4):
                        for md in range(KD):
                            t = 16 * q + md
                            s = md % 2
                            wge(v, sPE.h, pe0 + t + 1)
                            wge(v, sR[s].h, r0[s] + 16 * (t // 2 + 1))
                            if t >= 2:
                                wge(v, sSt.h, st0 + 16 * (t - 1))
                            pA = pb[0 + 2 * s]
                            pB = pb[1 + 2 * s]
                            v.scalar_tensor_tensor(out=nres[:, 1024 * s:1024 * s + 512],
                                                   in0=pA[:], scalar=v_sd[:, md:md + 1],
                                                   in1=resd[:, 1024 * s:1024 * s + 512],
                                                   op0=ALU.mult, op1=ALU.add)
                            i = v.scalar_tensor_tensor(out=nres[:, 1024 * s + 512:1024 * (s + 1)],
                                                       in0=pB[:], scalar=v_sd[:, md:md + 1],
                                                       in1=resd[:, 1024 * s + 512:1024 * (s + 1)],
                                                       op0=ALU.mult, op1=ALU.add)
                            sDVE.inc(i)

        def emit_proj(wdram, KC, MC, kind, srcres=None, dstres=None):
            with ExitStack() as px:
                psb = lambda n_, s_, d_=F32: px.enter_context(nc.sbuf_tensor(uname(n_), s_, d_))
                wib = psb("pj_wib", [128, 2 * KC * 128], BF16)
                gt = psb("pj_gt", [128, 2 * 1024], BF16)
                if kind == "out":
                    resd = psb("pj_resd", [128, 2 * 1024], F32)
                    nres = psb("pj_nres", [128, 2 * 1024], F32)
                    lnm_s = psb("pj_lnm", [128, T], F32)
                    lnr_s = psb("pj_lnr", [128, T], F32)
                    u1 = psb("pj_u1", [128, 1024], F32)
                    u3 = psb("pj_u3", [128, 1024], F32)

                sn0 = snap()
                wl0, pe0, act0, dve0, st0, sc0 = sWl.v, sPE.v, sACT.v, sDVE.v, sSt.v, sSc.v
                ld0 = sLd.v
                q0 = [sQ[0].v, sQ[1].v]
                r0 = [sR[0].v, sR[1].v]

                @block.gpsimd
                def _(gp):
                    bwait(gp, sn0)
                    for m in range(MC):
                        s = m % 2
                        if m >= 2:
                            wge(gp, sPE.h, pe0 + 4 * (m - 1))
                        i = gp.dma_start(
                            out=wib[:, KC * 128 * s:KC * 128 * (s + 1)].rearrange(
                                "p (k c) -> p k c", c=128),
                            in_=wdram[:, 128 * m:128 * (m + 1)].rearrange("(k p) c -> p k c", p=128))
                        sQ[s].inc(i, 16)

                @block.scalar
                def _(sc):
                    bwait(sc, sn0)
                    for m in range(MC):
                        if kind == "out":
                            for q in range(4):
                                t = 4 * m + q
                                s = t % 2
                                if t >= 2:
                                    wge(sc, sDVE.h, dve0 + t - 1)
                                i = sc.dma_start(out=resd[:, 1024 * s:1024 * (s + 1)],
                                                 in_=srcres[128 * m:128 * (m + 1),
                                                            1024 * q:1024 * (q + 1)])
                                sR[s].inc(i, 16)
                        if kind == "in" and m >= 12:
                            for q in range(4):
                                t = 4 * m + q
                                wge(sc, sPE.h, pe0 + t + 1)
                                wge(sc, sDVE.h, dve0 + t)
                                if t >= 2:
                                    wge(sc, sSt.h, st0 + 16 * (t - 1))
                                p = t % 2
                                pA, pB = pb[0 + 2 * p], pb[1 + 2 * p]
                                gs = 1024 * (t % 2)
                                zb = v_inb[:, m:m + 1]
                                sc.activation(out=gt[:, gs:gs + 512], in_=pA[:], func=AF.Silu,
                                              bias=zb)
                                i = sc.activation(out=gt[:, gs + 512:gs + 1024], in_=pB[:],
                                                  func=AF.Silu, bias=zb)
                                sDVE.inc(i)

                @block.tensor
                def _(te):
                    bwait(te, sn0)
                    for m in range(MC):
                        s = m % 2
                        wge(te, sQ[s].h, q0[s] + 16 * (m // 2 + 1))
                        for q in range(4):
                            t = 4 * m + q
                            if t >= 2:
                                wge(te, sDVE.h, dve0 + t - 1)
                            p = t % 2
                            pA, pB = pb[0 + 2 * p], pb[1 + 2 * p]
                            for k in range(KC):
                                wk = wib[:, KC * 128 * s + 128 * k:KC * 128 * s + 128 * (k + 1)]
                                x0 = xln[:, 4096 * k + 1024 * q:4096 * k + 1024 * q + 512]
                                x1 = xln[:, 4096 * k + 1024 * q + 512:4096 * k + 1024 * (q + 1)]
                                te.matmul(pA[:], wk, x0, start=(k == 0), stop=(k == KC - 1))
                                i = te.matmul(pB[:], wk, x1, start=(k == 0), stop=(k == KC - 1))
                            sPE.inc(i)

                @block.vector
                def _(v):
                    bwait(v, sn0)
                    for m in range(MC):
                        for q in range(4):
                            t = 4 * m + q
                            if kind == "in" and m >= 12:
                                continue
                            wge(v, sPE.h, pe0 + t + 1)
                            if t >= 2:
                                wge(v, sSt.h, st0 + 16 * (t - 1))
                            p = t % 2
                            pA, pB = pb[0 + 2 * p], pb[1 + 2 * p]
                            gs = 1024 * (t % 2)
                            if kind == "in":
                                bcol = v_inb[:, m:m + 1]
                                v.tensor_scalar(out=gt[:, gs:gs + 512], in0=pA[:], scalar1=bcol,
                                                scalar2=None, op0=ALU.add)
                                i = v.tensor_scalar(out=gt[:, gs + 512:gs + 1024], in0=pB[:],
                                                    scalar1=bcol, scalar2=None, op0=ALU.add)
                                sDVE.inc(i)
                            elif kind == "xp":
                                bcol = v_xpb[:, m:m + 1]
                                v.tensor_scalar(out=gt[:, gs:gs + 512], in0=pA[:], scalar1=bcol,
                                                scalar2=None, op0=ALU.add)
                                i = v.tensor_scalar(out=gt[:, gs + 512:gs + 1024], in0=pB[:],
                                                    scalar1=bcol, scalar2=None, op0=ALU.add)
                                sDVE.inc(i)
                            else:
                                wge(v, sR[t % 2].h, r0[t % 2] + 16 * (t // 2 + 1))
                                if t == 0:
                                    wge(v, sLd.h, ld0 + 32)
                                ns = 1024 * (t % 2)
                                lm = lnm_s[:, 1024 * q:1024 * (q + 1)]
                                lr = lnr_s[:, 1024 * q:1024 * (q + 1)]
                                v.tensor_tensor(out=u1[:], in0=resd[:, ns:ns + 1024],
                                                in1=lm, op=ALU.subtract)
                                v.tensor_tensor(out=u1[:], in0=u1[:], in1=lr, op=ALU.mult)
                                v.tensor_scalar(out=u3[:, 0:512], in0=pA[:],
                                                scalar1=rg_col[:, 0:1],
                                                scalar2=v_opb[:, m:m + 1],
                                                op0=ALU.mult, op1=ALU.add)
                                v.tensor_scalar(out=u3[:, 512:1024], in0=pB[:],
                                                scalar1=rg_col[:, 0:1],
                                                scalar2=v_opb[:, m:m + 1],
                                                op0=ALU.mult, op1=ALU.add)
                                v.tensor_tensor(out=u1[:], in0=u1[:], in1=u3[:], op=ALU.add)
                                i = v.tensor_tensor(out=nres[:, ns:ns + 1024], in0=u1[:],
                                                    in1=resd[:, ns:ns + 1024], op=ALU.add)
                                sDVE.inc(i)

                @block.sync
                def _(sync):
                    bwait(sync, sn0)
                    if kind == "out":
                        sLd.inc(sync.dma_start(out=lnm_s[:], in_=lnm_d), 16)
                        sLd.inc(sync.dma_start(out=lnr_s[:], in_=lnr_d), 16)
                    for m in range(MC):
                        for q in range(4):
                            t = 4 * m + q
                            wge(sync, sDVE.h, dve0 + t + 1)
                            gs = 1024 * (t % 2)
                            if kind == "in":
                                if m < 12:
                                    b = q // 2
                                    col = b * XPW + PADL + 1024 * (q % 2)
                                    dst = xpad[128 * m:128 * (m + 1), col:col + 1024]
                                    i = sync.dma_start(out=dst, in_=gt[:, gs:gs + 1024])
                                else:
                                    dst = siluzT[128 * (m - 12):128 * (m - 11),
                                                 1024 * q:1024 * (q + 1)]
                                    i = sync.dma_start(out=dst, in_=gt[:, gs:gs + 1024])
                            elif kind == "xp":
                                dst = xpT_s[128 * m:128 * (m + 1), 1024 * q:1024 * (q + 1)]
                                i = sync.dma_start(out=dst, in_=gt[:, gs:gs + 1024])
                            else:
                                ns = 1024 * (t % 2)
                                dst = dstres[128 * m:128 * (m + 1), 1024 * q:1024 * (q + 1)]
                                i = sync.dma_start(out=dst, in_=nres[:, ns:ns + 1024])
                            sSt.inc(i, 16)
                            wge(sync, sSt.h, sSt.v)

        def emit_conv():
            with ExitStack() as px:
                psb = lambda n_, s_, d_=F32: px.enter_context(nc.sbuf_tensor(uname(n_), s_, d_))
                xpt = psb("cv_x", [128, 2 * XPW], BF16)
                a1 = psb("cv_a1", [128, 2 * 2048], F32)
                a2 = psb("cv_a2", [128, 2 * 2048], F32)

                sn0 = snap()
                ld0, dve0, po0, act0 = sLd.v, sDVE.v, sPool.v, sACT.v
                q0 = [sQ[0].v, sQ[1].v]

                @block.sync
                def _(sync):
                    bwait(sync, sn0)
                    for c in range(KI):
                        for b in range(NB):
                            i = 2 * c + b
                            s = i % 2
                            if i >= 2:
                                wge(sync, sDVE.h, dve0 + 2 * (i - 1))
                                wge(sync, sPool.h, po0 + i - 1)
                            ii = sync.dma_start(out=xpt[:, XPW * s:XPW * (s + 1)],
                                                in_=xpad[128 * c:128 * (c + 1),
                                                         b * XPW:(b + 1) * XPW])
                            sQ[s].inc(ii, 16)

                @block.vector
                def _(v):
                    bwait(v, sn0)
                    for c in range(KI):
                        for b in range(NB):
                            i = 2 * c + b
                            s = i % 2
                            wge(v, sQ[s].h, q0[s] + 16 * (i // 2 + 1))
                            if i >= 2:
                                wge(v, sACT.h, act0 + i - 1)
                                wge(v, sPool.h, po0 + i - 1)
                            xs = xpt[:, XPW * s:XPW * (s + 1)]
                            v.tensor_scalar(out=a1[:, 2048 * s:2048 * (s + 1)],
                                            in0=xs[:, 0:2048], scalar1=v_cw[:, 4 * c:4 * c + 1],
                                            scalar2=None, op0=ALU.mult)
                            ii = v.scalar_tensor_tensor(out=a1[:, 2048 * s:2048 * (s + 1)],
                                                        in0=xs[:, 1:2049],
                                                        scalar=v_cw[:, 4 * c + 1:4 * c + 2],
                                                        in1=a1[:, 2048 * s:2048 * (s + 1)],
                                                        op0=ALU.mult, op1=ALU.add)
                            sDVE.inc(ii)
                            v.tensor_scalar(out=a2[:, 2048 * s:2048 * (s + 1)],
                                            in0=xs[:, 2:2050], scalar1=v_cw[:, 4 * c + 2:4 * c + 3],
                                            scalar2=None, op0=ALU.mult)
                            ii = v.scalar_tensor_tensor(out=a2[:, 2048 * s:2048 * (s + 1)],
                                                        in0=xs[:, 3:2051],
                                                        scalar=v_cw[:, 4 * c + 3:4 * c + 4],
                                                        in1=a2[:, 2048 * s:2048 * (s + 1)],
                                                        op0=ALU.mult, op1=ALU.add)
                            sDVE.inc(ii)

                @block.gpsimd
                def _(gp):
                    bwait(gp, sn0)
                    for c in range(KI):
                        for b in range(NB):
                            i = 2 * c + b
                            s = i % 2
                            wge(gp, sDVE.h, dve0 + 2 * (i + 1))
                            ii = gp.tensor_tensor(out=a2[:, 2048 * s:2048 * (s + 1)],
                                                  in0=a1[:, 2048 * s:2048 * (s + 1)],
                                                  in1=a2[:, 2048 * s:2048 * (s + 1)], op=ALU.add)
                            sPool.inc(ii)

                @block.scalar
                def _(sc):
                    bwait(sc, sn0)
                    for c in range(KI):
                        for b in range(NB):
                            i = 2 * c + b
                            s = i % 2
                            wge(sc, sPool.h, po0 + i + 1)
                            dst = xln[:, 4096 * c + 2048 * b:4096 * c + 2048 * (b + 1)]
                            ii = sc.activation(out=dst, in_=a2[:, 2048 * s:2048 * (s + 1)],
                                               func=AF.Silu, bias=v_cb[:, c:c + 1])
                            sACT.inc(ii)

        def emit_scan(np_cap=None, j_cap=None):
            NP = np_cap if np_cap is not None else NCH // 2
            JC = j_cap if j_cap is not None else 48
            KC = JC // 2
            PE_SA_U, ACT_SA_U, DVE_SA_U, PO_SA_U, LD_SA_U = 28, 23, 16, 3, 144
            PE_PP = 2 * PE_SA_U + 48 * 4
            ACT_PP = 2 * ACT_SA_U + 24 * 7
            DVE_PP = 2 * DVE_SA_U + 24 * 12
            PO_PP = 2 * PO_SA_U + 48 * 2
            LD_PP = 2 * LD_SA_U

            def PEj(j, e):
                k, u = j // 2, j % 2
                if e == 0:
                    return 2 * PE_SA_U + 8 * k + 1 + u
                if e == 1:
                    return 2 * PE_SA_U + 8 * k + 3 + u
                return 2 * PE_SA_U + 8 * k + 3 + 2 * u + e

            def DVEj(j, e):
                k, u = j // 2, j % 2
                if e == 0:
                    return 2 * DVE_SA_U + 12 * k + 1 + u
                return 2 * DVE_SA_U + 12 * k + 2 + 5 * u + e

            def ACTrt(j):
                return 2 * ACT_SA_U + 7 * (j // 2) + 1 + (j % 2)

            def ACTdexp(k):
                return 2 * ACT_SA_U + 7 * k + 3

            def ACTw2c(j):
                return 2 * ACT_SA_U + 7 * (j // 2) + (4 if j % 2 == 0 else 6)

            def ACTdco(j):
                return 2 * ACT_SA_U + 7 * (j // 2) + (5 if j % 2 == 0 else 7)

            def POj(j, e):
                return 2 * PO_SA_U + 2 * j + e + 1

            TB = 49152

            with ExitStack() as px:
                psb = lambda n_, s_, d_=F32: px.enter_context(nc.sbuf_tensor(uname(n_), s_, d_))
                xpc = psb("sn_xpc", [128, 2 * P3R], BF16)
                szc = psb("sn_szc", [128, 2 * DI], BF16)
                sqt = psb("sn_sqt", [128, 3072], BF16)
                cstat = psb("sn_cstat", [24, 384], F32)
                ntmp = psb("sn_ntmp", [24, 128], F32)
                nrm = psb("sn_nrm", [24, 256], F32)
                rr = psb("sn_rr", [24, 256], F32)
                lnrc = psb("sn_lnrc", [24, 128], F32)
                dta = psb("sn_dta", [24, 128], F32)
                dtb = psb("sn_dtb", [24, 128], F32)
                ltl = psb("sn_ltl", [24, 128], F32)
                att_t = psb("sn_at", [24, 128], F32)
                adec = psb("sn_adec", [24, 128], F32)
                oma_b = psb("sn_omab", [24, 128], BF16)
                pet = psb("sn_pet", [24, 128], F32)
                pt1 = psb("sn_pt1", [24, 128], F32)
                pt2 = psb("sn_pt2", [24, 128], F32)
                b1t = psb("sn_b1", [24, 128], F32)
                tht = psb("sn_th", [24, 128], F32)
                alf = psb("sn_alf", [24, 128], F32)
                omal_b = psb("sn_omal", [24, 128], BF16)
                lat = psb("sn_la", [24, 128], F32)
                cum24 = psb("sn_cum", [24, 128], F32)
                cumP = psb("sn_cumP", [24, 128], F32)
                ecr = psb("sn_ecr", [24, 128], F32)
                st48 = psb("sn_st48", [64, 128], F32)
                ctm = psb("sn_ctm", [128, 2 * 64], F32)
                qr48 = psb("sn_qr48", [64, 128], BF16)
                qrowS = psb("sn_qrow", [1, 3072], BF16)
                xbs0 = psb("sn_xbs0", [128, 3072], BF16)
                h0 = psb("sn_h0", [64, 2 * 1536], BF16)
                rtp = psb("sn_rtp", [128, 2 * 128], F32)
                dxp = psb("sn_dxp", [128, 2 * 128], BF16)
                wtb = psb("sn_wtb", [128, 2 * 128], BF16)
                smb = psb("sn_smb", [128, 2 * 128], BF16)
                btl = psb("sn_btl", [128, 2 * 64], BF16)
                ddt = psb("sn_ddt", [64, 2 * 64], BF16)
                cn = psb("sn_cn", [128, 2 * 128], BF16)
                ctl = psb("sn_ctl", [64, 2 * 128], BF16)
                w2c = psb("sn_w2c", [128, 2], F32)
                dco = psb("sn_dco", [128, 2], F32)

                def tailR(u):
                    return xln[:, TB + 3072 * u:TB + 3072 * (u + 1)]

                def tailE(u):
                    return xln[:, TB + 6144 + 3072 * u:TB + 6144 + 3072 * (u + 1)]

                def xbs(u):
                    return xbs0[:] if u == 0 else xln[:, TB + 12288:TB + 15360]

                sn0 = snap()
                ld0, pe0, act0, dve0, po0 = sLd.v, sPE.v, sACT.v, sDVE.v, sPool.v
                q0 = [sQ[0].v, sQ[1].v]

                M_PE = lambda pp, idx: pe0 + pp * PE_PP + idx
                M_ACT = lambda pp, idx: act0 + pp * ACT_PP + idx
                M_DVE = lambda pp, idx: dve0 + pp * DVE_PP + idx
                M_PO = lambda pp, idx: po0 + pp * PO_PP + idx
                M_LD = lambda pp, idx: ld0 + pp * LD_PP + idx
                M_Q = lambda u, pp, h: q0[u] + 384 * pp + 16 * (h + 1)

                def colof(pp, u):
                    return 2048 * u + 128 * pp

                def pRv(u):
                    return pb[4 + u][:, 0:128]

                def pGv(u):
                    return pb[6 + u][:, 128:256]

                def pYv(u, hb):
                    return pb[6 + u][hb:hb + 64, 256:384]

                def pHv(u):
                    return pb[4 + u][0:64, 384:448]

                @block.sync
                def _(sync):
                    bwait(sync, sn0)
                    for pp in range(NP):
                        for u in range(2):
                            col0 = colof(pp, u)
                            if pp >= 1:
                                wge(sync, sPE.h, M_PE(pp - 1, PEj(46 + u, 1)))
                                wge(sync, sPool.h, M_PO(pp - 1, POj(46 + u, 1)))
                            src = xpT_s[:, col0:col0 + 128]
                            for j0, j1 in ((0, 7), (7, 13), (13, 19), (19, 25)):
                                ii = sync.dma_start(
                                    out=xpc[:, P3R * u + j0 * 128:P3R * u + j1 * 128].rearrange(
                                        "p (j c) -> p j c", c=128),
                                    in_=src.rearrange("(j p) c -> p j c", p=128)[:, j0:j1])
                                sLd.inc(ii, 16)
                            if pp >= 1:
                                wge(sync, sDVE.h, M_DVE(pp - 1, DVEj(46 + u, 5)))
                            for j0, j1 in ((0, 6), (6, 12)):
                                ii = sync.dma_start(
                                    out=szc[:, DI * u + j0 * 128:DI * u + j1 * 128].rearrange(
                                        "p (j c) -> p j c", c=128),
                                    in_=siluzT[:, col0:col0 + 128].rearrange(
                                        "(j p) c -> p j c", p=128)[:, j0:j1])
                                sLd.inc(ii, 16)
                            wge(sync, sPool.h, M_PO(pp, PO_SA_U * u + 3))
                            ii = sync.dma_start(out=qrow_d, in_=qr48[:])
                            sLd.inc(ii, 16)
                            wge(sync, sLd.h, sLd.v)
                            ri0 = 2 * u
                            for s, r0, r1 in ((0, 0, 24), (1, 32, 56)):
                                ri = ri0 + s
                                if ri >= 1:
                                    up, sp = divmod(ri - 1, 2)
                                    wge(sync, sPE.h,
                                                 M_PE(pp, PE_SA_U * up + 16 + 6 * (sp + 1)))
                                elif pp >= 1:
                                    wge(sync, sPE.h, M_PE(pp - 1, 2 * PE_SA_U))
                                ii = sync.dma_start(
                                    out=qrowS[:],
                                    in_=qrow_d[r0:r1].rearrange("a b -> (a b)")[None, :])
                                sLd.inc(ii, 16)
                        for k in range(KC):
                            j = 2 * k
                            if k == 0:
                                wge(sync, sDVE.h, M_DVE(pp, 16))
                                if pp >= 1:
                                    wge(sync, sPE.h, M_PE(pp - 1, PEj(46, 3)))
                                    wge(sync, sDVE.h, M_DVE(pp - 1, DVEj(46, 2)))
                            else:
                                wge(sync, sPE.h, M_PE(pp, PEj(j - 2, 3)))
                                wge(sync, sDVE.h, M_DVE(pp, DVEj(j - 2, 2)))
                            ii = sync.dma_start_transpose(
                                out=smb[:, 0:128], in_=xbs(0)[:, 128 * k:128 * (k + 1)])
                            sQ[0].inc(ii, 16)

                @block.gpsimd
                def _(gp):
                    bwait(gp, sn0)
                    for pp in range(NP):
                        if pp >= 1:
                            wge(gp, sPE.h, M_PE(pp - 1, PE_PP))
                        for u in range(2):
                            ub_d = DVE_SA_U * u
                            ub_a = ACT_SA_U * u
                            wge(gp, sLd.h, M_LD(pp, LD_SA_U * u + 96))
                            if pp >= 1:
                                wge(gp, sQ[u].h, q0[u] + 384 * pp)
                            if pp == 0:
                                ii = gp.memset(h0[:, 1536 * u:1536 * (u + 1)], 0.0)
                                sPool.inc(ii)
                            else:
                                sPool.inc(gp.nop())
                            for hh in range(24):
                                srcp = xpc[:, P3R * u + 128 * (hh // 2):
                                           P3R * u + 128 * (hh // 2) + 128]
                                ii = gp.tensor_copy(
                                    out=xbs(u)[64:128, 128 * hh:128 * (hh + 1)],
                                    in_=srcp[64 * (hh % 2):64 * (hh % 2) + 64, :])
                            sPool.inc(ii)
                            wge(gp, sDVE.h, M_DVE(pp, ub_d + 11))
                            wge(gp, sACT.h, M_ACT(pp, ub_a + 11))
                            gp.tensor_copy(out=qr48[0:24, :],
                                           in_=rr[:, 128:256])
                            ii = gp.tensor_copy(out=qr48[32:56, :],
                                                in_=ecr[:])
                            sPool.inc(ii)
                        wge(gp, sACT.h, M_ACT(pp, 2 * ACT_SA_U))
                        for j in range(JC):
                            u_, h_ = j % 2, j // 2
                            hb = 64 * (h_ % 2)
                            csrc = xpc[hb:hb + 64,
                                       P3R * u_ + 128 * (12 + h_ // 2):
                                       P3R * u_ + 128 * (12 + h_ // 2) + 128]
                            if j >= 2:
                                wge(gp, sPE.h, M_PE(pp, PEj(j - 2, 1)))
                            ii = gp.tensor_tensor(
                                out=cn[hb:hb + 64, 128 * u_:128 * (u_ + 1)],
                                in0=csrc, in1=tailR(u_)[hb:hb + 64, 128 * h_:128 * (h_ + 1)],
                                op=ALU.mult)
                            sPool.inc(ii)
                            if j >= 2:
                                wge(gp, sPE.h, M_PE(pp, PEj(j - 2, 2)))
                            ii = gp.tensor_tensor(
                                out=ctl[:, 128 * u_:128 * (u_ + 1)],
                                in0=csrc, in1=tailE(u_)[hb:hb + 64, 128 * h_:128 * (h_ + 1)],
                                op=ALU.mult)
                            sPool.inc(ii)

                @block.scalar
                def _(sc):
                    bwait(sc, sn0)
                    one24 = ccol[0:24, CC_ONE:CC_ONE + 1]
                    for pp in range(NP):
                        if pp >= 1:
                            wge(sc, sPool.h, M_PO(pp - 1, PO_PP))
                            wge(sc, sDVE.h, M_DVE(pp - 1, 2 * DVE_SA_U))
                        for u in range(2):
                            o = 128 * u
                            o2 = 256 * u
                            ub_d = DVE_SA_U * u
                            ub_p = PE_SA_U * u
                            wge(sc, sDVE.h, M_DVE(pp, ub_d + 2))
                            sACT.inc(sc.activation(out=ntmp[:],
                                                   in_=cstat[:, 0:128],
                                                   func=AF.Ln))
                            wge(sc, sACT.h, sACT.v)
                            sACT.inc(sc.activation(out=nrm[:, 0:128],
                                                   in_=ntmp[:], func=AF.Exp,
                                                   scale=0.5))
                            sACT.inc(sc.activation(out=ntmp[:],
                                                   in_=cstat[:, 128:256],
                                                   func=AF.Ln))
                            wge(sc, sACT.h, sACT.v)
                            sACT.inc(sc.activation(out=nrm[:, 128:256],
                                                   in_=ntmp[:], func=AF.Exp,
                                                   scale=0.5))
                            wge(sc, sDVE.h, M_DVE(pp, ub_d + 4))
                            sACT.inc(sc.activation(out=lnrc[:],
                                                   in_=rr[:, 128:256],
                                                   func=AF.Ln))
                            wge(sc, sDVE.h, M_DVE(pp, ub_d + 5))
                            sACT.inc(sc.activation(out=ltl[:],
                                                   in_=dtb[:], func=AF.Exp))
                            wge(sc, sACT.h, sACT.v)
                            sACT.inc(sc.activation(out=ltl[:],
                                                   in_=ltl[:], func=AF.Ln,
                                                   bias=one24))
                            wge(sc, sDVE.h, M_DVE(pp, ub_d + 6))
                            sACT.inc(sc.activation(out=adec[:],
                                                   in_=att_t[:], func=AF.Exp))
                            wge(sc, sDVE.h, M_DVE(pp, ub_d + 9))
                            sACT.inc(sc.activation(out=tht[:],
                                                   in_=b1t[:], func=AF.Tanh))
                            wge(sc, sDVE.h, M_DVE(pp, ub_d + 10))
                            sACT.inc(sc.activation(out=lat[:],
                                                   in_=alf[:], func=AF.Ln))
                            wge(sc, sDVE.h, M_DVE(pp, ub_d + 13))
                            if u == 1:
                                wge(sc, sPool.h, M_PO(pp, 3))
                            sACT.inc(sc.activation(out=ecr[:],
                                                   in_=cumP[:], func=AF.Exp))
                            for s in range(2):
                                for r in range(6):
                                    wge(sc, sPE.h, M_PE(pp, ub_p + 16 + 6 * s + r + 1))
                                    bank = pb[1 + (6 * s + r) % 3]
                                    tl = tailR(u) if s == 0 else tailE(u)
                                    ii = sc.copy(out=tl[:, 512 * r:512 * (r + 1)],
                                                 in_=bank[:, 0:512])
                                    sACT.inc(ii)
                        if pp >= 1:
                            wge(sc, sDVE.h, M_DVE(pp - 1, DVE_PP))
                        for k in range(KC):
                            j0, j1 = 2 * k, 2 * k + 1
                            if k == 0:
                                wge(sc, sDVE.h, M_DVE(pp, 2 * DVE_SA_U))
                            for j in (j0, j1):
                                u_, h_ = j % 2, j // 2
                                if j >= 2:
                                    wge(sc, sPE.h, M_PE(pp, PEj(j - 2, 0)))
                                ii = sc.activation(out=rtp[:, 128 * u_:128 * (u_ + 1)],
                                                   in_=maskpos[:], func=AF.Identity,
                                                   scale=0.0,
                                                   bias=ctm[:, 64 * u_ + h_:64 * u_ + h_ + 1])
                                sACT.inc(ii)
                            if k == 0:
                                wge(sc, sDVE.h, M_DVE(pp, 32))
                                if pp >= 1:
                                    wge(sc, sPE.h, M_PE(pp - 1, PEj(47, 3)))
                            else:
                                wge(sc, sPE.h, M_PE(pp, PEj(j1 - 2, 3)))
                                wge(sc, sDVE.h, M_DVE(pp, DVEj(j1 - 2, 2)))
                            ii = sc.dma_start_transpose(
                                out=smb[:, 128:256], in_=xbs(1)[:, 128 * k:128 * (k + 1)])
                            sQ[1].inc(ii, 16)
                            wge(sc, sDVE.h, M_DVE(pp, DVEj(j1, 0)))
                            if k >= 1:
                                wge(sc, sDVE.h, M_DVE(pp, DVEj(j1 - 2, 1)))
                            sACT.inc(sc.activation(out=dxp[:], in_=rtp[:], func=AF.Exp))
                            if k >= 1:
                                wge(sc, sDVE.h, M_DVE(pp, DVEj(j0 - 2, 2)))
                            sACT.inc(sc.activation(out=w2c[:, 0:1], in_=rtp[:, 127:128],
                                                   func=AF.Exp))
                            if k >= 1:
                                wge(sc, sDVE.h, M_DVE(pp, DVEj(j0 - 2, 3)))
                            sACT.inc(sc.activation(out=dco[:, 0:1], in_=pRv(0)[:, 127:128],
                                                   func=AF.Exp))
                            if k >= 1:
                                wge(sc, sDVE.h, M_DVE(pp, DVEj(j1 - 2, 2)))
                            sACT.inc(sc.activation(out=w2c[:, 1:2], in_=rtp[:, 255:256],
                                                   func=AF.Exp))
                            if k >= 1:
                                wge(sc, sDVE.h, M_DVE(pp, DVEj(j1 - 2, 3)))
                            sACT.inc(sc.activation(out=dco[:, 1:2], in_=pRv(1)[:, 127:128],
                                                   func=AF.Exp))

                @block.tensor
                def _(te):
                    bwait(te, sn0)
                    for pp in range(NP):
                        wge(te, sDVE.h, dve0 + pp * DVE_PP)
                        wge(te, sACT.h, act0 + pp * ACT_PP)
                        wge(te, sPool.h, po0 + pp * PO_PP)
                        for u in range(2):
                            col0 = colof(pp, u)
                            ub_d = DVE_SA_U * u
                            ub_a = ACT_SA_U * u
                            wge(te, sLd.h, M_LD(pp, LD_SA_U * u + 96))
                            wge(te, sDVE.h, M_DVE(pp, ub_d + 1))
                            if u == 1:
                                wge(te, sDVE.h, M_DVE(pp, 15))
                            for jx in range(KI):
                                ii = te.matmul(pb[0][0:24, 0:128],
                                               indicT_b[:, 24 * jx:24 * (jx + 1)],
                                               sqt[:, 128 * jx:128 * (jx + 1)],
                                               start=(jx == 0), stop=(jx == KI - 1))
                            sPE.inc(ii)
                            for jx in range(KI):
                                ii = te.matmul(pb[0][0:24, 128:256],
                                               indicT_b[:, 24 * jx:24 * (jx + 1)],
                                               sqt[:, 1536 + 128 * jx:1536 + 128 * (jx + 1)],
                                               start=(jx == 0), stop=(jx == KI - 1))
                            sPE.inc(ii)
                            for jx in range(KI):
                                ii = te.matmul(pb[0][0:24, 256:384],
                                               indicT64_b[:, 24 * jx:24 * (jx + 1)],
                                               xln[:, 4096 * jx + col0:4096 * jx + col0 + 128],
                                               start=(jx == 0), stop=(jx == KI - 1))
                            sPE.inc(ii)
                            wge(te, sDVE.h, M_DVE(pp, ub_d + 14))
                            ii = te.transpose(pb[0][:, 384 + 64 * u:448 + 64 * u],
                                              st48[:],
                                              ident_f[0:64, 0:64])
                            sPE.inc(ii)
                            wge(te, sDVE.h, M_DVE(pp, ub_d + 11))
                            if u == 1:
                                wge(te, sACT.h, M_ACT(pp, ACT_SA_U))
                            for jx in range(KI):
                                ii = te.matmul(pb[1 + jx // 4][:, 128 * (jx % 4):
                                                               128 * (jx % 4 + 1)],
                                               indic_b[:, 128 * jx:128 * (jx + 1)],
                                               omal_b[:],
                                               start=True, stop=True)
                                sPE.inc(ii)
                            wge(te, sDVE.h, M_DVE(pp, ub_d + 16))
                            for s in range(2):
                                wge(te, sLd.h, M_LD(pp, LD_SA_U * u + 112 + 16 * (s + 1)))
                                for r in range(6):
                                    g = 12 * u + 6 * s + r
                                    if g >= 3 and 6 * s + r >= 3:
                                        up, tp = divmod(g - 3, 12)
                                        wge(te, sACT.h,
                                                   M_ACT(pp, ACT_SA_U * up + 11 + tp + 1))
                                    bank = pb[1 + (6 * s + r) % 3]
                                    ii = te.matmul(
                                        bank[:, 0:512],
                                        ones_b[0:1, 0:128],
                                        qrowS[0:1, 512 * r:512 * (r + 1)],
                                        start=True, stop=True)
                                    sPE.inc(ii)
                        wge(te, sLd.h, M_LD(pp, 2 * LD_SA_U))
                        wge(te, sPool.h, M_PO(pp, 2 * PO_SA_U))
                        wge(te, sDVE.h, M_DVE(pp, 2 * DVE_SA_U))
                        for k in range(KC):
                            for j in (2 * k, 2 * k + 1):
                                u_, h_ = j % 2, j // 2
                                wge(te, sACT.h, M_ACT(pp, ACTrt(j)))
                                if j >= 2:
                                    wge(te, sDVE.h, M_DVE(pp, DVEj(j - 2, 0)))
                                    wge(te, sDVE.h, M_DVE(pp, DVEj(j - 2, 4)))
                                    wge(te, sACT.h, M_ACT(pp, ACTdco(j - 2)))
                                ii = te.transpose(pRv(u_), rtp[:, 128 * u_:128 * (u_ + 1)],
                                                  ident_f[:])
                                sPE.inc(ii)
                            for j in (2 * k, 2 * k + 1):
                                u_, h_ = j % 2, j // 2
                                hb = 64 * (h_ % 2)
                                wge(te, sPool.h, M_PO(pp, POj(j, 0)))
                                if j >= 2:
                                    wge(te, sDVE.h, M_DVE(pp, DVEj(j - 2, 1)))
                                    wge(te, sDVE.h, M_DVE(pp, DVEj(j - 2, 5)))
                                bsl = xpc[hb:hb + 64,
                                          P3R * u_ + 128 * (h_ // 2):
                                          P3R * u_ + 128 * (h_ // 2) + 128]
                                ii = te.matmul(pGv(u_), bsl,
                                               cn[hb:hb + 64, 128 * u_:128 * (u_ + 1)],
                                               start=True, stop=True)
                                sPE.inc(ii)
                            for j in (2 * k, 2 * k + 1):
                                u_, h_ = j % 2, j // 2
                                hb = 64 * (h_ % 2)
                                wge(te, sDVE.h, M_DVE(pp, DVEj(j, 1)))
                                wge(te, sQ[u_].h, M_Q(u_, pp, h_))
                                wge(te, sPool.h, M_PO(pp, POj(j, 1)))
                                if j >= 2:
                                    wge(te, sDVE.h, M_DVE(pp, DVEj(j - 2, 5)))
                                te.matmul(pYv(u_, hb), smb[:, 128 * u_:128 * u_ + 64],
                                          wtb[:, 128 * u_:128 * (u_ + 1)],
                                          start=True, stop=False)
                                ii = te.matmul(pYv(u_, hb),
                                               h0[:, 1536 * u_ + 64 * h_:
                                                  1536 * u_ + 64 * (h_ + 1)],
                                               ctl[:, 128 * u_:128 * (u_ + 1)],
                                               start=False, stop=True)
                                sPE.inc(ii)
                                wge(te, sDVE.h, M_DVE(pp, DVEj(j, 2)))
                                wge(te, sDVE.h, M_DVE(pp, DVEj(j, 3)))
                                if j >= 2:
                                    wge(te, sDVE.h, M_DVE(pp, DVEj(j - 2, 4)))
                                te.matmul(pHv(u_), btl[:, 64 * u_:64 * (u_ + 1)],
                                          smb[:, 128 * u_:128 * u_ + 64],
                                          start=True, stop=False)
                                ii = te.matmul(pHv(u_), ddt[:, 64 * u_:64 * (u_ + 1)],
                                               h0[:, 1536 * u_ + 64 * h_:
                                                  1536 * u_ + 64 * (h_ + 1)],
                                               start=False, stop=True)
                                sPE.inc(ii)

                @block.vector
                def _(v):
                    bwait(v, sn0)
                    for pp in range(NP):
                        for u in range(2):
                            col0 = colof(pp, u)
                            o = 128 * u
                            o2 = 256 * u
                            o3 = 384 * u
                            ub_p = PE_SA_U * u
                            ub_a = ACT_SA_U * u
                            wge(v, sLd.h, M_LD(pp, LD_SA_U * u + 96))
                            if u == 1:
                                wge(v, sPE.h, M_PE(pp, 2))
                            elif pp >= 1:
                                wge(v, sPE.h, M_PE(pp - 1, PE_SA_U + 2))
                            if pp >= 1:
                                wge(v, sQ[u].h, q0[u] + 384 * pp)
                            ii = v.tensor_tensor(out=sqt[:],
                                                 in0=xpc[:, P3R * u:P3R * u + 3072],
                                                 in1=xpc[:, P3R * u:P3R * u + 3072],
                                                 op=ALU.mult)
                            sDVE.inc(ii)
                            wge(v, sPE.h, M_PE(pp, ub_p + 3))
                            if u == 1:
                                wge(v, sACT.h, M_ACT(pp, 4))
                            v.tensor_copy(out=cstat[:, 0:128], in_=pb[0][0:24, 0:128])
                            v.tensor_copy(out=cstat[:, 128:256],
                                          in_=pb[0][0:24, 128:256])
                            ii = v.tensor_copy(out=cstat[:, 256:384],
                                               in_=pb[0][0:24, 256:384])
                            sDVE.inc(ii)
                            wge(v, sACT.h, M_ACT(pp, ub_a + 4))
                            ii = v.tensor_scalar(out=nrm[:],
                                                 in0=nrm[:],
                                                 scalar1=1e-12, scalar2=None, op0=ALU.max)
                            sDVE.inc(ii)
                            wge(v, sDVE.h, sDVE.v)
                            if u == 1:
                                wge(v, sPool.h, M_PO(pp, 3))
                                wge(v, sACT.h, M_ACT(pp, 5))
                            ii = v.reciprocal(out=rr[:], in_=nrm[:])
                            sDVE.inc(ii)
                            dsl = xpc[0:24, P3R * u + 3072:P3R * u + 3200]
                            v.tensor_scalar(out=dta[:], in0=dsl, scalar1=0.0,
                                            scalar2=None, op0=ALU.max)
                            v.tensor_scalar(out=dtb[:], in0=dsl, scalar1=-1.0,
                                            scalar2=None, op0=ALU.mult)
                            ii = v.tensor_tensor(out=dtb[:], in0=dsl,
                                                 in1=dtb[:], op=ALU.min)
                            sDVE.inc(ii)
                            wge(v, sACT.h, M_ACT(pp, ub_a + 7))
                            v.tensor_tensor(out=dta[:], in0=dta[:],
                                            in1=ltl[:], op=ALU.add)
                            ii = v.tensor_scalar(out=att_t[:],
                                                 in0=dta[:],
                                                 scalar1=v_hd[:, 4:5], scalar2=None,
                                                 op0=ALU.mult)
                            sDVE.inc(ii)
                            wge(v, sACT.h, M_ACT(pp, ub_a + 8))
                            ii = v.tensor_scalar(out=oma_b[:],
                                                 in0=adec[:],
                                                 scalar1=-1.0, scalar2=1.0,
                                                 op0=ALU.mult, op1=ALU.add)
                            sDVE.inc(ii)
                            v.tensor_tensor(out=pt1[:], in0=oma_b[:],
                                            in1=cstat[:, 256:384], op=ALU.mult)
                            v.tensor_tensor(out=pt2[:], in0=pt1[:],
                                            in1=pt1[:], op=ALU.mult)
                            v.tensor_tensor(out=pt1[:], in0=rr[:, 0:128],
                                            in1=rr[:, 0:128], op=ALU.mult)
                            v.scalar_tensor_tensor(out=pt1[:],
                                                   in0=pt1[:],
                                                   scalar=1.0 / 64, in1=cstat[:, 0:128],
                                                   op0=ALU.mult, op1=ALU.mult)
                            ii = v.tensor_tensor(out=pet[:], in0=pt2[:],
                                                 in1=pt1[:], op=ALU.mult)
                            sDVE.inc(ii)
                            ii = v.tensor_scalar(out=b1t[:], in0=pet[:],
                                                 scalar1=v_hd[:, 3:4], scalar2=v_hd[:, 6:7],
                                                 op0=ALU.mult, op1=ALU.mult)
                            sDVE.inc(ii)
                            wge(v, sACT.h, M_ACT(pp, ub_a + 9))
                            v.tensor_scalar(out=alf[:], in0=tht[:],
                                            scalar1=0.0, scalar2=None, op0=ALU.max)
                            v.tensor_scalar(out=alf[:], in0=alf[:],
                                            scalar1=v_hd[:, 7:8], scalar2=v_hd[:, 5:6],
                                            op0=ALU.mult, op1=ALU.add)
                            ii = v.tensor_scalar(out=alf[:], in0=alf[:],
                                                 scalar1=0.01, scalar2=0.999,
                                                 op0=ALU.max, op1=ALU.min)
                            sDVE.inc(ii)
                            ii = v.tensor_scalar(out=omal_b[:],
                                                 in0=alf[:],
                                                 scalar1=-1.0, scalar2=1.0,
                                                 op0=ALU.mult, op1=ALU.add)
                            sDVE.inc(ii)
                            wge(v, sACT.h, M_ACT(pp, ub_a + 10))
                            ii = v.tensor_tensor_scan(out=cum24[:],
                                                      data0=lat[:], data1=zer24[:],
                                                      initial=0.0, op0=ALU.add, op1=ALU.add)
                            sDVE.inc(ii)
                            wge(v, sACT.h, M_ACT(pp, ub_a + 5))
                            ii = v.tensor_tensor(out=cumP[:],
                                                 in0=cum24[:],
                                                 in1=lnrc[:], op=ALU.add)
                            sDVE.inc(ii)
                            if u == 1:
                                wge(v, sPE.h, M_PE(pp, 4))
                            elif pp >= 1:
                                wge(v, sPE.h, M_PE(pp - 1, PE_SA_U + 4))
                            v.tensor_copy(out=st48[0:24, :], in_=cum24[:])
                            ii = v.tensor_copy(out=st48[32:56, :],
                                               in_=rr[:, 0:128])
                            sDVE.inc(ii)
                            wge(v, sPE.h, M_PE(pp, ub_p + 4))
                            ii = v.tensor_copy(out=ctm[:, 64 * u:64 * (u + 1)],
                                               in_=pb[0][:, 384 + 64 * u:448 + 64 * u])
                            sDVE.inc(ii)
                            wge(v, sPE.h, M_PE(pp, ub_p + 16))
                            for hh in range(24):
                                jq = hh // 2
                                pd = pb[1 + jq // 4]
                                ii = v.tensor_tensor(
                                    out=xbs(u)[0:64, 128 * hh:128 * (hh + 1)],
                                    in0=xln[64 * (hh % 2):64 * (hh % 2) + 64,
                                            4096 * jq + col0:4096 * jq + col0 + 128],
                                    in1=pd[64 * (hh % 2):64 * (hh % 2) + 64,
                                           128 * (jq % 4):128 * (jq % 4 + 1)],
                                    op=ALU.mult)
                            sDVE.inc(ii)
                        for k in range(KC):
                            for j in (2 * k, 2 * k + 1):
                                u_, h_ = j % 2, j // 2
                                wge(v, sPE.h, M_PE(pp, PEj(j, 0)))
                                if j >= 2:
                                    wge(v, sACT.h, M_ACT(pp, ACTw2c(j - 2)))
                                ii = v.scalar_tensor_tensor(
                                    out=rtp[:, 128 * u_:128 * (u_ + 1)],
                                    in0=pRv(u_),
                                    scalar=ctm[:, 64 * u_ + h_:64 * u_ + h_ + 1],
                                    in1=maskpos[:],
                                    op0=ALU.subtract, op1=ALU.subtract)
                                sDVE.inc(ii)
                            for j in (2 * k, 2 * k + 1):
                                u_, h_ = j % 2, j // 2
                                hb = 64 * (h_ % 2)
                                wge(v, sACT.h, M_ACT(pp, ACTdexp(k)))
                                wge(v, sPE.h, M_PE(pp, PEj(j, 1)))
                                if j >= 2:
                                    wge(v, sPE.h, M_PE(pp, PEj(j - 2, 2)))
                                ii = v.scalar_tensor_tensor(
                                    out=wtb[:, 128 * u_:128 * (u_ + 1)],
                                    in0=dxp[:, 128 * u_:128 * (u_ + 1)],
                                    scalar=ctm[:, 64 * u_ + 32 + h_:64 * u_ + 33 + h_],
                                    in1=pGv(u_),
                                    op0=ALU.mult, op1=ALU.mult)
                                sDVE.inc(ii)
                                wge(v, sQ[u_].h, M_Q(u_, pp, h_))
                                wge(v, sACT.h, M_ACT(pp, ACTw2c(j)))
                                if j >= 2:
                                    wge(v, sPE.h, M_PE(pp, PEj(j - 2, 3)))
                                ii = v.tensor_scalar(
                                    out=btl[:, 64 * u_:64 * (u_ + 1)],
                                    in0=smb[:, 128 * u_ + 64:128 * (u_ + 1)],
                                    scalar1=ctm[:, 64 * u_ + 32 + h_:64 * u_ + 33 + h_],
                                    scalar2=w2c[:, u_:u_ + 1],
                                    op0=ALU.mult, op1=ALU.mult)
                                sDVE.inc(ii)
                                wge(v, sACT.h, M_ACT(pp, ACTdco(j)))
                                if j >= 2:
                                    wge(v, sPE.h, M_PE(pp, PEj(j - 2, 3)))
                                ii = v.tensor_scalar(out=ddt[:, 64 * u_:64 * (u_ + 1)],
                                                     in0=ident_f[0:64, 0:64],
                                                     scalar1=dco[0:64, u_:u_ + 1],
                                                     scalar2=None, op0=ALU.mult)
                                sDVE.inc(ii)
                                wge(v, sPE.h, M_PE(pp, PEj(j, 3)))
                                ii = v.tensor_copy(
                                    out=h0[:, 1536 * u_ + 64 * h_:1536 * u_ + 64 * (h_ + 1)],
                                    in_=pHv(u_))
                                sDVE.inc(ii)
                                ii = v.tensor_tensor(
                                    out=xln[hb:hb + 64,
                                            4096 * (h_ // 2) + colof(pp, u_):
                                            4096 * (h_ // 2) + colof(pp, u_) + 128],
                                    in0=pYv(u_, hb),
                                    in1=szc[hb:hb + 64,
                                            DI * u_ + 128 * (h_ // 2):
                                            DI * u_ + 128 * (h_ // 2) + 128],
                                    op=ALU.mult)
                                sDVE.inc(ii)

        if phase_on(1):
            emit_ln(hsT, rms=False)
        if phase_on(2):
            emit_gateup(wdprep=True)
        if phase_on(3):
            emit_down(hsT, resA)
        if phase_on(4):
            emit_ln(resA, rms=True)
        if phase_on(5):
            emit_proj(inT, KD, 24, "in")
        if phase_on(6):
            emit_conv()
        if phase_on(7):
            emit_proj(xpTw, KI, KP, "xp")
        if phase_on(8):
            emit_scan(np_cap=scan_np, j_cap=scan_jc)
        if phase_on(9):
            emit_proj(opT, KI, KD, "out", srcres=resA, dstres=resB)
        if phase_on(10):
            emit_ln(resB, rms=False)
        if phase_on(11):
            emit_gateup(wdprep=False)
        if phase_on(12):
            emit_down(resB, out)

        if dev:
            sn1 = snap()

            @block.sync
            def _(sync):
                bwait(sync, sn1)
                for k in range(KD):
                    i = sync.dma_start(out=xln_d[128 * k:128 * (k + 1), :],
                                       in_=xln[:, 4096 * k:4096 * (k + 1)])
                    sSt.inc(i, 16)
                for k in range(KI):
                    i = sync.dma_start(out=xcy_d[128 * k:128 * (k + 1), :],
                                       in_=xln[:, 4096 * k:4096 * (k + 1)])
                    sSt.inc(i, 16)

        sn_end = snap()

        @block.sync
        def _(sync):
            bwait(sync, sn_end)

    return nc


def _prep_host(inputs):
    f32 = np.float32
    att = {k[8:]: v for k, v in inputs.items() if k.startswith("L0_attn_")}
    ffn = {k[7:]: v for k, v in inputs.items() if k.startswith("L0_ffn_")}
    hs = np.asarray(att["hidden_states"], f32)

    com = {}
    com["wgT"] = np.ascontiguousarray(np.asarray(ffn["wq_gate"], np.int32).T)
    com["wuT"] = np.ascontiguousarray(np.asarray(ffn["wq_up"], np.int32).T)
    com["wdT"] = np.ascontiguousarray(np.asarray(ffn["wq_down"], np.int32).T)
    com["sgate"] = np.ascontiguousarray(np.asarray(ffn["s_gate"], f32).reshape(KH, 128).T)
    com["sup"] = np.ascontiguousarray(np.asarray(ffn["s_up"], f32).reshape(KH, 128).T)
    com["sdown"] = np.ascontiguousarray(np.asarray(ffn["s_down"], f32).reshape(KD, 128).T)
    com["inT"] = np.ascontiguousarray(np.asarray(att["in_proj_w"], f32).T)
    com["inb"] = np.ascontiguousarray(np.asarray(att["in_proj_b"], f32).reshape(24, 128).T)

    xw = np.asarray(att["x_proj_w"], f32)
    xb = np.asarray(att["x_proj_b"], f32)
    perm = np.zeros(P3R, np.int64)
    wnew = np.zeros((P3R, DI), f32)
    bnew = np.zeros(P3R, f32)
    for h in range(H):
        base = h * 129
        wnew[64 * h:64 * (h + 1)] = xw[base + 1:base + 65]
        bnew[64 * h:64 * (h + 1)] = xb[base + 1:base + 65]
        wnew[1536 + 64 * h:1536 + 64 * (h + 1)] = xw[base + 65:base + 129]
        bnew[1536 + 64 * h:1536 + 64 * (h + 1)] = xb[base + 65:base + 129]
        wnew[3072 + h] = xw[base]
        bnew[3072 + h] = xb[base]
    com["xpTw"] = np.ascontiguousarray(wnew.T)
    com["xpb"] = np.ascontiguousarray(bnew.reshape(KP, 128).T)

    com["opT"] = np.ascontiguousarray(np.asarray(att["out_proj_w"], f32).T)
    com["opb"] = np.ascontiguousarray(np.asarray(att["out_proj_b"], f32).reshape(KD, 128).T)
    cw = np.asarray(att["conv_w"], f32).reshape(DI, 4)
    com["convw"] = np.ascontiguousarray(cw.reshape(KI, 128, 4).transpose(1, 0, 2).reshape(128, KI * 4))
    com["convb"] = np.ascontiguousarray(np.asarray(att["conv_b"], f32).reshape(KI, 128).T)
    com["normw"] = np.ascontiguousarray(np.asarray(att["norm_w"], f32).reshape(KD, 128).T)
    com["headv"] = np.ascontiguousarray(np.stack(
        [np.asarray(att["A_log"], f32), np.asarray(att["log2_alpha_base"], f32),
         np.asarray(att["log2_beta"], f32), np.asarray(att["surprise_ema"], f32)], axis=1))
    com["resg"] = np.asarray(att["res_gate"], f32).reshape(1, 1)

    com["ones_h"] = np.ones((128, 128), f32)
    ind = np.zeros((24, DI), f32)
    for h in range(H):
        ind[h, 64 * h:64 * (h + 1)] = 1.0
    com["indic_h"] = ind
    it = np.zeros((128, KI * 24), f32)
    it64 = np.zeros((128, KI * 24), f32)
    for j in range(KI):
        for p in range(128):
            h = 2 * j + p // 64
            it[p, 24 * j + h] = 1.0
            it64[p, 24 * j + h] = 1.0 / 64
    com["indicT_h"] = it
    com["indicT64_h"] = it64
    tau = np.arange(128)[:, None]
    tt = np.arange(128)[None, :]
    com["maskpos_h"] = np.where(tau > tt, 1e30, 0.0).astype(f32)
    com["ident_h"] = np.eye(128, dtype=f32)
    cc = np.zeros((128, NCC), f32)
    cc[:, CC_M128] = -128.0
    cc[:, CC_ONE] = 1.0
    cc[:, CC_LNEPS] = LN_EPS
    cc[:, CC_RMSB] = LN_EPS * RMS_EPS
    com["ccol_h"] = cc
    return hs, com


def kernel(**inputs):
    hs, com = _prep_host(inputs)
    nc = build_program()
    in_maps = []
    for c in range(8):
        m = dict(com)
        m["hsT"] = np.ascontiguousarray(
            np.concatenate([hs[2 * c].T, hs[2 * c + 1].T], axis=1))
        in_maps.append(m)
    res = run_bass_kernel_spmd(nc, in_maps, core_ids=list(range(8)))
    outs = np.empty((16, L, D), np.float32)
    for c in range(8):
        o = res.results[c]["out"]
        outs[2 * c] = o[:, 0:L].T
        outs[2 * c + 1] = o[:, L:2 * L].T
    return outs
